# version 13
# baseline (speedup 1.0000x reference)
import numpy as np
import concourse.bass as bass
import concourse.mybir as mybir
from concourse.bass_utils import run_bass_kernel_spmd
from concourse.bass_core_rust import IndirectOffsetOnAxis
from contextlib import ExitStack

F32 = mybir.dt.float32
I32 = mybir.dt.int32
BF16 = mybir.dt.bfloat16
AF = mybir.ActivationFunctionType
ALU = mybir.AluOpType
AX = mybir.AxisListType

D = 1024
DT = 8
NP = 8
NE = 8
DFF = 4096
NF = 32
CH = 512
DH = 64
EPS = 1e-5
NEG = -1e30
SCALE = 1024.0 ** -0.5


class Plan:

    ENGINES = ("tensor", "vector", "scalar", "gpsimd", "sync")

    def __init__(self, nc, ctx):
        self.nc = nc
        self.pend = {e: [] for e in self.ENGINES}
        self.cnt = {}
        self.h = {}
        for name in (["LQ", "HQ", "OQ", "AQ", "WQ", "PE", "VE", "SE", "GE",
                      "GQ2", "TQ", "DPE", "DSE", "DVE", "EBPE", "EBGQ", "EBSY",
                      "EBSE", "EBVE", "SDONE", "WS",
                      "GS", "GG0", "GG1", "GC0", "GC1", "GY0", "GY1",
                      "AQ0", "AQ1", "TQ0", "TQ1"]):
            self.h[name] = ctx.enter_context(nc.semaphore(name))
            self.cnt[name] = 0

    def op(self, eng, fn, wait=(), inc=None, amt=1):
        w = [(s, v) for (s, v) in wait if v > 0]
        for s, v in w:
            assert v <= self.cnt[s], f"wait on future {s}:{v}>{self.cnt[s]}"
        after = None
        if inc is not None:
            self.cnt[inc] += amt
            after = self.cnt[inc]

        def run(En, w=w, fn=fn, inc=inc, amt=amt):
            for s, v in w:
                En.wait_ge(self.h[s], v)
            r = fn(En)
            if inc is not None:
                r.then_inc(self.h[inc], amt)

        self.pend[eng].append(run)
        return after

    def waitonly(self, eng, wait):
        w = [(s, v) for (s, v) in wait if v > 0]
        if not w:
            return
        for s, v in w:
            assert v <= self.cnt[s]

        def run(En, w=w):
            for s, v in w:
                En.wait_ge(self.h[s], v)

        self.pend[eng].append(run)

    def at(self, s):
        return self.cnt[s]

    def snap(self):
        return dict(self.cnt)

    def fence(self, snap, engines=ENGINES):
        for e in engines:
            self.waitonly(e, list(snap.items()))

    def flush(self, name=None):
        with self.nc.Block(name) as b:
            for eng in self.ENGINES:
                ops = self.pend[eng]
                if ops:
                    getattr(b, eng)(lambda En, ops=ops: [f(En) for f in ops])
        self.pend = {e: [] for e in self.ENGINES}


def build(S=4096, debug_x1=False):
    assert S % 512 == 0
    C = S // CH
    NT = S // 128
    nc = bass.Bass("TRN2", target_bir_lowering=False)
    try:
        nc.disable_value_cache = True
    except Exception:
        pass
    x1dbg = (nc.dram_tensor("x1dbg", [D, S], F32, kind="ExternalOutput")
             if debug_x1 else None)
    basedbg = (nc.dram_tensor("basedbg", [D, S], F32, kind="ExternalOutput")
               if debug_x1 else None)
    oadbg = (nc.dram_tensor("oadbg", [128, NP * S], BF16, kind="ExternalOutput")
             if debug_x1 else None)
    h2dbg = (nc.dram_tensor("h2dbg", [8, S], BF16, kind="ExternalOutput")
             if debug_x1 else None)

    xT = nc.dram_tensor("xT", [D, S], F32, kind="ExternalInput")
    wq = nc.dram_tensor("wq", [D, D], F32, kind="ExternalInput")
    wk = nc.dram_tensor("wk", [D, D], F32, kind="ExternalInput")
    wv = nc.dram_tensor("wv", [D, D], F32, kind="ExternalInput")
    wo = nc.dram_tensor("wo", [D, D], F32, kind="ExternalInput")
    wr = nc.dram_tensor("wr", [D, NE], F32, kind="ExternalInput")
    brr = nc.dram_tensor("brr", [1, NE], F32, kind="ExternalInput")
    w1 = nc.dram_tensor("w1", [NE, D, DFF], F32, kind="ExternalInput")
    b1l = nc.dram_tensor("b1l", [128, NF * NE], F32, kind="ExternalInput")
    w2 = nc.dram_tensor("w2", [NE, DFF, D], F32, kind="ExternalInput")
    b2 = nc.dram_tensor("b2", [NE, D], F32, kind="ExternalInput")
    pos2 = nc.dram_tensor("pos2", [128, S], F32, kind="ExternalInput")
    masksd = nc.dram_tensor("masksd", [128, 4 * CH], F32, kind="ExternalInput")
    identd = nc.dram_tensor("identd", [128, 128], F32, kind="ExternalInput")
    h2d = nc.dram_tensor("h2d", [D, S], BF16,
                         **({"kind": "ExternalOutput"} if debug_x1 else {}))
    w1bf = nc.dram_tensor("w1bf", [NE, D, DFF], BF16)
    w2bf = nc.dram_tensor("w2bf", [NE, DFF, D], BF16)
    dbgkind = {"kind": "ExternalOutput"} if debug_x1 else {}
    h2tm = nc.dram_tensor("h2tm", [S + 1, D], BF16, **dbgkind)
    ytm = nc.dram_tensor("ytm", [S + 1, D], BF16, **dbgkind)
    srcd = nc.dram_tensor("srcd", [1, S + 1024], mybir.dt.int32, **dbgkind)
    gateD = nc.dram_tensor("gateD", [1, S + 1], F32, **dbgkind)
    intd = nc.dram_tensor("intd", [8, 2], mybir.dt.int32,
                          **({"kind": "ExternalOutput"} if debug_x1 else {}))
    ltri = nc.dram_tensor("ltri", [8, 8], F32, kind="ExternalInput")
    outT = nc.dram_tensor("outT", [D, S], F32, kind="ExternalOutput")

    r3 = lambda t: t.rearrange("(dt pp) s -> pp dt s", pp=128)

    with ExitStack() as ctx:
        P = Plan(nc, ctx)
        sb = lambda n, sh, dt: ctx.enter_context(nc.sbuf_tensor(n, sh, dt))

        onesrow_f = sb("onesrow_f", [1, 128], F32)
        onesrow_b = sb("onesrow_b", [1, 128], BF16)
        onescol_f = sb("onescol_f", [128, 1], F32)
        onescol_b = sb("onescol_b", [128, 1], BF16)
        ident = sb("ident", [128, 128], BF16)
        identf = sb("identf", [128, 128], F32)
        maskgT = sb("maskgT", [8, S], BF16)
        introw = sb("introw", [1, 32], I32)
        pw0 = ctx.enter_context(nc.psum_tensor("pw0", [128, 1024], F32))
        pw1 = ctx.enter_context(nc.psum_tensor("pw1", [128, 1024], F32))
        pb4 = ctx.enter_context(nc.psum_tensor("pb4", [128, 512], F32))
        pb5 = ctx.enter_context(nc.psum_tensor("pb5", [128, 512], F32))
        pb6 = ctx.enter_context(nc.psum_tensor("pb6", [128, 512], F32))
        pb7 = ctx.enter_context(nc.psum_tensor("pb7", [128, 512], F32))

        P.op("gpsimd", lambda g: g.memset(onesrow_f[:, :], 1.0), inc="GE")
        P.op("gpsimd", lambda g: g.memset(onesrow_b[:, :], 1.0), inc="GE")
        P.op("gpsimd", lambda g: g.memset(onescol_f[:, :], 1.0), inc="GE")
        ge_ones = P.op("gpsimd", lambda g: g.memset(onescol_b[:, :], 1.0), inc="GE")
        P.op("gpsimd", lambda g: g.dma_start(ident[:, :], identd[:, :]),
             inc="WQ", amt=16)
        wq_ident = P.op("sync", lambda sy: sy.dma_start(identf[:, :], identd[:, :]),
                        inc="LQ", amt=16)

        st = {"ve_rows": 0, "pe_sq": [0, 0]}

        def ln_chunk(c, xchunk, Hbig, sq, tmpf, rowbuf, ready):
            for d in range(DT):
                sqs = sq[:, (d % 2) * CH:(d % 2 + 1) * CH]
                xs = xchunk[:, d * CH:(d + 1) * CH]
                se_sq = P.op("scalar", lambda sc, sqs=sqs, xs=xs: sc.activation(
                    sqs, xs, AF.Square),
                    wait=(list(ready) if d == 0 else []) + [("PE", st["pe_sq"][d % 2])],
                    inc="SE")
                w0 = (list(ready) + [("GE", ge_ones), ("VE", st["ve_rows"])]
                      if d == 0 else [])
                P.op("tensor", lambda t, xs=xs, d=d: t.matmul(
                    pw1[0:1, 0:CH], onescol_f[:, :], xs,
                    start=(d == 0), stop=(d == DT - 1)), wait=w0)
                st["pe_sq"][d % 2] = P.op("tensor", lambda t, sqs=sqs, d=d: t.matmul(
                    pw1[0:1, 512:512 + CH], onescol_f[:, :], sqs,
                    start=(d == 0), stop=(d == DT - 1)),
                    wait=[("SE", se_sq)], inc="PE")
            pe_stats = P.at("PE")
            ME, MS, M2, VA, SD, RS, RM, R0 = (rowbuf[0:1, i * 512:(i + 1) * 512]
                                              for i in range(8))
            w = P.op("vector", lambda v: v.tensor_scalar_mul(ME, pw1[0:1, 0:CH], 1.0 / D),
                     wait=[("PE", pe_stats)], inc="VE")
            w = P.op("vector", lambda v: v.tensor_scalar_mul(MS, pw1[0:1, 512:512 + CH], 1.0 / D),
                     wait=[("VE", w)], inc="VE")
            w = P.op("vector", lambda v: v.tensor_mul(M2, ME, ME),
                     wait=[("VE", w)], inc="VE")
            ve_var = P.op("vector", lambda v: v.scalar_tensor_tensor(
                VA, MS, EPS, M2, ALU.add, ALU.subtract), wait=[("VE", w)], inc="VE")
            se_sd = P.op("scalar", lambda sc: sc.activation(SD, VA, AF.Sqrt),
                         wait=[("VE", ve_var)], inc="SE")
            w = P.op("vector", lambda v: v.reciprocal(R0, SD),
                     wait=[("SE", se_sd)], inc="VE")
            w = P.op("vector", lambda v: v.tensor_mul(M2, SD, R0),
                     wait=[("VE", w)], inc="VE")
            w = P.op("vector", lambda v: v.tensor_scalar(M2, M2, -1.0, 2.0,
                                                         ALU.mult, ALU.add),
                     wait=[("VE", w)], inc="VE")
            w = P.op("vector", lambda v: v.tensor_mul(RS, R0, M2),
                     wait=[("VE", w)], inc="VE")
            st["ve_rows"] = P.op("vector", lambda v: v.scalar_tensor_tensor(
                RM, ME, -1.0, RS, ALU.mult, ALU.mult), wait=[("VE", w)], inc="VE")
            P.op("tensor", lambda t: t.matmul(pb4[:, 0:CH], onesrow_f[0:1, :],
                                              RS, start=True, stop=True),
                 wait=[("VE", st["ve_rows"])])
            pe_bc = P.op("tensor", lambda t: t.matmul(pb5[:, 0:CH], onesrow_f[0:1, :],
                                                      RM, start=True, stop=True),
                         inc="PE")
            ve_h = 0
            for d in range(DT):
                xs = xchunk[:, d * CH:(d + 1) * CH]
                wm = P.op("vector", lambda v, xs=xs: v.tensor_mul(
                    tmpf[:, :], pb4[:, 0:CH], xs),
                    wait=([("PE", pe_bc)] if d == 0 else []) + ([("VE", ve_h)] if d else []),
                    inc="VE")
                ve_h = P.op("vector", lambda v, d=d: v.tensor_add(
                    Hbig[:, d * S + c * CH: d * S + (c + 1) * CH],
                    pb5[:, 0:CH], tmpf[:, :]), wait=[("VE", wm)], inc="VE")
            return ve_h

        with ExitStack() as abc:
            sbH = lambda n, sh, dt: abc.enter_context(nc.sbuf_tensor(n, sh, dt))
            Hbig = sbH("Hbig", [128, DT * S], BF16)
            rowbuf = sbH("rowbuf", [1, 4096], F32)
            Hs = lambda d, a, b: Hbig[:, d * S + a: d * S + b]

            with ExitStack() as pa:
                sbA = lambda n, sh, dt: pa.enter_context(nc.sbuf_tensor(n, sh, dt))
                xchunk = sbA("xchunkA", [128, DT * CH], F32)
                sq = sbA("sqA", [128, 2 * CH], F32)
                tmpf = sbA("tmpfA", [128, CH], F32)
                ve_h = 0
                for c in range(C):
                    lq = P.op("sync", lambda sy, c=c: sy.dma_start(
                        xchunk.rearrange("p (d c) -> p d c", c=CH),
                        r3(xT)[:, :, c * CH:(c + 1) * CH]),
                        wait=[("VE", ve_h)], inc="LQ", amt=16)
                    ve_h = ln_chunk(c, xchunk, Hbig, sq, tmpf, rowbuf, [("LQ", lq)])
                P.flush("phaseA")
            endA = P.snap()

            with ExitStack() as bcs:
                sbBC = lambda n, sh, dt: bcs.enter_context(nc.sbuf_tensor(n, sh, dt))
                out_all = sbBC("out_all", [128, NP * S], BF16)

                with ExitStack() as pbs:
                    sbB = lambda n, sh, dt: pbs.enter_context(nc.sbuf_tensor(n, sh, dt))
                    QP = sbB("QP", [128, S], BF16)
                    KP = sbB("KP", [128, S], BF16)
                    VP = sbB("VP", [128, NT * 130], BF16)
                    wqp = sbB("wqp", [128, D], BF16)
                    wkp = sbB("wkp", [128, D], BF16)
                    wvp = sbB("wvp", [128, D], BF16)
                    U = sbB("U", [128, 4 * 1024], BF16)
                    pos2b = sbB("pos2b", [128, S], BF16)
                    masksb = sbB("masksb", [128, 4 * CH], BF16)
                    bc_sb = sbB("bc_sb", [64, 2 * CH], F32)
                    rrb = rowbuf[0:1, 0:CH].bitcast(BF16)

                    P.fence(endA, engines=("gpsimd",))
                    P.op("gpsimd", lambda g: g.dma_start(pos2b[:, :], pos2[:, :]),
                         inc="WQ", amt=16)
                    wq_pm = P.op("gpsimd", lambda g: g.dma_start(masksb[:, :], masksd[:, :]),
                                 inc="WQ", amt=16)
                    ge_vones = P.op("gpsimd", lambda g: g.memset(
                        VP.rearrange("p (t w) -> p t w", w=130)[:, :, 64:65], 1.0),
                        inc="GE")
                    ge_vones = P.op("gpsimd", lambda g: g.memset(
                        VP.rearrange("p (t w) -> p t w", w=130)[:, :, 129:130], 1.0),
                        inc="GE")
                    P.fence(endA, engines=("tensor", "vector", "scalar", "sync"))

                    pe_lastuse_w = endA["PE"]
                    widefree_s = [("VE", endA["VE"]), ("VE", endA["VE"])]
                    pe_uslot_s = [[endA["PE"]] * 2, [endA["PE"]] * 2]
                    pe_bcr_s = [endA["PE"], endA["PE"]]
                    ve_outall_s = [endA["VE"], endA["VE"]]
                    ve_evacq = endA["VE"]
                    ve_evack = endA["VE"]
                    ve_pb67 = {0: endA["VE"], 1: endA["VE"]}
                    for p in range(NP):
                        wq_w = None
                        for (dst, src) in ((wqp, wq), (wkp, wk), (wvp, wv)):
                            wq_w = P.op("gpsimd", lambda g, dst=dst, src=src, p=p: g.dma_start(
                                dst.rearrange("pp (d m) -> pp d m", m=128),
                                src.rearrange("(d pp) m -> pp d m", pp=128)[:, :, p * 128:(p + 1) * 128]),
                                wait=[("PE", pe_lastuse_w)], inc="WQ", amt=16)
                        P.op("gpsimd", lambda g, p=p: g.dma_start(w1bf[p], w1[p]),
                             inc="WQ", amt=16)
                        P.op("gpsimd", lambda g, p=p: g.dma_start(w2bf[p], w2[p]),
                             inc="WQ", amt=16)
                        for c in range(C):
                            peq = None
                            for d in range(DT):
                                peq = P.op("tensor", lambda t, d=d, c=c: t.matmul(
                                    pw0[:, 0:CH], wqp[:, d * 128:(d + 1) * 128],
                                    Hs(d, c * CH, (c + 1) * CH),
                                    start=(d == 0), stop=(d == DT - 1)),
                                    wait=([("WQ", wq_w), widefree_s[0],
                                           ("VE", ve_evacq)] if d == 0 else []),
                                    inc="PE" if d == DT - 1 else None)
                            ve_evacq = P.op("vector", lambda v, c=c: v.tensor_add(
                                QP[:, c * CH:(c + 1) * CH], pw0[:, 0:CH],
                                pos2b[:, c * CH:(c + 1) * CH]),
                                wait=[("PE", peq)], inc="VE")
                            pek = None
                            for d in range(DT):
                                pek = P.op("tensor", lambda t, d=d, c=c: t.matmul(
                                    pw1[:, 0:CH], wkp[:, d * 128:(d + 1) * 128],
                                    Hs(d, c * CH, (c + 1) * CH),
                                    start=(d == 0), stop=(d == DT - 1)),
                                    wait=([widefree_s[1], ("VE", ve_evack)]
                                          if d == 0 else []),
                                    inc="PE" if d == DT - 1 else None)
                            ve_evack = P.op("vector", lambda v, c=c: v.tensor_add(
                                KP[:, c * CH:(c + 1) * CH], pw1[:, 0:CH],
                                pos2b[:, c * CH:(c + 1) * CH]),
                                wait=[("PE", pek)], inc="VE")
                        for tt in range(NT):
                            bk = (pb6, pb7)[tt % 2]
                            pev = None
                            for d in range(DT):
                                pev = P.op("tensor", lambda t, d=d, tt=tt, bk=bk: t.matmul(
                                    bk[:, 0:128], Hs(d, tt * 128, (tt + 1) * 128),
                                    wvp[:, d * 128:(d + 1) * 128],
                                    start=(d == 0), stop=(d == DT - 1)),
                                    wait=[("VE", ve_pb67[tt % 2])] if d == 0 else [],
                                    inc="PE" if d == DT - 1 else None)
                            wv1 = P.op("vector", lambda v, tt=tt, bk=bk: v.tensor_copy(
                                VP[:, tt * 130: tt * 130 + 64], bk[:, 0:64]),
                                wait=[("PE", pev)], inc="VE")
                            ve_pb67[tt % 2] = P.op("vector", lambda v, tt=tt, bk=bk: v.tensor_copy(
                                VP[:, tt * 130 + 65: tt * 130 + 129], bk[:, 64:128]),
                                wait=[("VE", wv1)], inc="VE")
                        pe_lastuse_w = P.at("PE")
                        widefree_s = [("VE", ve_evacq), ("VE", ve_evack)]

                        for j in range(C):
                            kn = 4 * j + 4
                            npair = kn // 2
                            segs = ({"hb": 0, "wide": pw0, "pvb": pb4, "bcb": pb6,
                                     "uo": 0, "rr": rrb[0:1, 0:CH],
                                     "bcs": bc_sb[0:64, 0:CH]},
                                    {"hb": 64, "wide": pw1, "pvb": pb5, "bcb": pb7,
                                     "uo": 2048, "rr": rrb[0:1, CH:2 * CH],
                                     "bcs": bc_sb[0:64, CH:2 * CH]})
                            se_exp = [{}, {}]

                            def emit_spair(pi):
                                masked = 2 * pi >= 4 * j
                                pe_close = [None, None]
                                for half in (0, 1):
                                    kt = 2 * pi + half
                                    for sg in (0, 1):
                                        S_ = segs[sg]
                                        wide = S_["wide"]
                                        hb = S_["hb"]
                                        w = []
                                        if half == 0:
                                            w.append(widefree_s[sg])
                                            if pi == 0 and j == 0 and sg == 0:
                                                w.append(("WQ", wq_pm))
                                        last = (half == 1) and not masked
                                        mm = P.op("tensor", lambda t, kt=kt, hb=hb, j=j, masked=masked, wide=wide, half=half: t.matmul(
                                            wide[:, half * CH:(half + 1) * CH],
                                            KP[hb:hb + 64, kt * 128:(kt + 1) * 128],
                                            QP[hb:hb + 64, j * CH:(j + 1) * CH],
                                            start=True, stop=not masked), wait=w,
                                            inc="PE" if last else None)
                                        if last:
                                            pe_close[sg] = mm
                                if masked:
                                    for half in (0, 1):
                                        kt = 2 * pi + half
                                        for sg in (0, 1):
                                            wide = segs[sg]["wide"]
                                            mm = P.op("tensor", lambda t, r=kt - 4 * j, wide=wide, half=half: t.matmul(
                                                wide[:, half * CH:(half + 1) * CH],
                                                ident[:, :],
                                                masksb[:, r * CH:(r + 1) * CH],
                                                start=False, stop=True),
                                                inc="PE" if half == 1 else None)
                                            if half == 1:
                                                pe_close[sg] = mm
                                for sg in (0, 1):
                                    S_ = segs[sg]
                                    wide = S_["wide"]
                                    se_exp[sg][pi] = P.op("scalar", lambda sc, wide=wide, uo=S_["uo"] + (pi % 2) * 1024: sc.activation(
                                        U[:, uo:uo + 1024], wide[:, 0:1024],
                                        AF.Exp, scale=SCALE),
                                        wait=[("PE", pe_close[sg]), ("PE", pe_uslot_s[sg][pi % 2])],
                                        inc="SE")
                                    widefree_s[sg] = ("SE", se_exp[sg][pi])

                            def emit_pvpair(sg, pi):
                                S_ = segs[sg]
                                hb = S_["hb"]
                                pvb = S_["pvb"]
                                for half in (0, 1):
                                    kt = 2 * pi + half
                                    w = [("SE", se_exp[sg][pi])] if half == 0 else []
                                    if kt == 0:
                                        w.append(("VE", ve_outall_s[sg]))
                                        w.append(("GE", ge_vones))
                                    P.op("tensor", lambda t, kt=kt, hb=hb, kn=kn, pvb=pvb, uo=S_["uo"] + (pi % 2) * 1024, half=half: t.matmul(
                                        pvb[0:65, 0:CH],
                                        VP[:, kt * 130 + 65 * (hb // 64): kt * 130 + 65 * (hb // 64) + 65],
                                        U[:, uo + half * CH: uo + (half + 1) * CH],
                                        start=(kt == 0), stop=(kt == kn - 1)),
                                        wait=w,
                                        inc="PE" if half == 1 else None)
                                pe_uslot_s[sg][pi % 2] = P.at("PE")

                            emit_spair(0)
                            for pi in range(1, npair):
                                emit_spair(pi)
                                emit_pvpair(0, pi - 1)
                                emit_pvpair(1, pi - 1)
                            emit_pvpair(0, npair - 1)
                            emit_pvpair(1, npair - 1)
                            for sg in (0, 1):
                                S_ = segs[sg]
                                def _rr(v, S_=S_):
                                    with nc.allow_low_precision(
                                            reason="bf16 softmax normalizer"):
                                        return v.reciprocal(S_["rr"],
                                                            S_["pvb"][64:65, 0:CH])
                                ve_rr = P.op("vector", _rr,
                                             wait=[("PE", max(pe_uslot_s[sg])),
                                                   ("PE", pe_bcr_s[sg])], inc="VE")
                                pe_bcr_s[sg] = P.op("tensor", lambda t, S_=S_: t.matmul(
                                    S_["bcb"][0:64, 0:CH], onesrow_b[0:1, 0:64],
                                    S_["rr"], start=True, stop=True),
                                    wait=[("VE", ve_rr), ("VE", ve_pb67[sg])],
                                    inc="PE")
                                wbc = P.op("vector", lambda v, S_=S_: v.tensor_copy(
                                    S_["bcs"], S_["bcb"][0:64, 0:CH]),
                                    wait=[("PE", pe_bcr_s[sg])], inc="VE")
                                ve_pb67[sg] = wbc
                                ve_outall_s[sg] = P.op("vector", lambda v, p=p, S_=S_, j=j: v.tensor_mul(
                                    out_all[S_["hb"]:S_["hb"] + 64,
                                            p * S + j * CH: p * S + (j + 1) * CH],
                                    S_["pvb"][0:64, 0:CH], S_["bcs"]),
                                    wait=[("VE", wbc)], inc="VE")
                    if oadbg is not None:
                        P.op("sync", lambda sy: sy.dma_start(
                            oadbg[:, :], out_all[:, :]),
                            wait=[("VE", P.at("VE"))], inc="HQ", amt=16)
                    P.flush("phaseB")
                endB = P.snap()

                with ExitStack() as pc:
                    sbC = lambda n, sh, dt: pc.enter_context(nc.sbuf_tensor(n, sh, dt))
                    xchunk = sbC("xchunkC", [128, DT * CH], F32)
                    sq = sbC("sqC", [128, 2 * CH], F32)
                    tmpf = sbC("tmpfC", [128, CH], F32)
                    wosb = sbC("wosb", [128, NP * D], BF16)
                    b2sb = sbC("b2sb", [8, D], BF16)
                    lg_sb = sbC("lg_sb", [128, NE], F32)
                    h2fp = sbC("h2fp", [128, DT * 128], F32)
                    wrf = sbC("wrf", [128, DT * NE], F32)
                    brf = sbC("brf", [1, NE], F32)
                    e8_sb = sbC("e8_sb", [128, NE], F32)
                    mgf = sbC("mgf", [128, NE], F32)
                    statc = sbC("statc", [128, 8], F32)
                    ttileC = sbC("ttileC", [128, 2 * D], BF16)

                    P.fence(endB, engines=("gpsimd",))
                    P.op("gpsimd", lambda g: g.dma_start(
                        wosb.rearrange("pp (pr m) -> pp pr m", m=D),
                        wo.rearrange("(pr pp) m -> pp pr m", pp=128)),
                        inc="WQ", amt=16)
                    wq_c = P.op("gpsimd", lambda g: g.dma_start(b2sb[:, :], b2[:, :]),
                                inc="WQ", amt=16)
                    P.fence(endB, engines=("tensor", "vector", "scalar", "sync"))
                    P.op("sync", lambda sy: sy.dma_start(
                        wrf.rearrange("pp (d m) -> pp d m", m=NE),
                        wr.rearrange("(d pp) m -> pp d m", pp=128)),
                        inc="LQ", amt=16)
                    lq_wrf = P.op("sync", lambda sy: sy.dma_start(brf[:, :], brr[:, :]),
                                  inc="LQ", amt=16)

                    st["ve_rows"] = endB["VE"]
                    st["pe_sq"] = [endB["PE"], endB["PE"]]
                    ve_done = 0
                    oq_prev = 0
                    ve_lgevac = 0
                    ve_ptbf = 0
                    ve_x1 = {0: endB["VE"], 1: endB["VE"]}
                    hq_relay = {0: 0, 1: 0}
                    tq_rel = {0: 0, 1: 0}
                    psO = lambda k: pw0[:, (k % 2) * 512:(k % 2) * 512 + CH]
                    for c in range(C):
                        lq = P.op("sync", lambda sy, c=c: sy.dma_start(
                            xchunk.rearrange("p (d c) -> p d c", c=CH),
                            r3(xT)[:, :, c * CH:(c + 1) * CH]),
                            wait=[("VE", ve_done), ("OQ", oq_prev)], inc="LQ", amt=16)
                        for d in range(DT):
                            peo = None
                            for pr in range(NP):
                                peo = P.op("tensor", lambda t, d=d, pr=pr, c=c: t.matmul(
                                    psO(d),
                                    wosb[:, pr * D + d * 128: pr * D + (d + 1) * 128],
                                    out_all[:, pr * S + c * CH: pr * S + (c + 1) * CH],
                                    start=(pr == 0), stop=(pr == NP - 1)),
                                    wait=([("WQ", wq_c), ("VE", ve_x1[d % 2])]
                                          if pr == 0 else []),
                                    inc="PE" if pr == NP - 1 else None)
                            ve_x1[d % 2] = P.op("vector", lambda v, d=d: v.tensor_add(
                                xchunk[:, d * CH:(d + 1) * CH], psO(d),
                                xchunk[:, d * CH:(d + 1) * CH]),
                                wait=[("PE", peo), ("LQ", lq)], inc="VE")
                        if x1dbg is not None:
                            P.op("sync", lambda sy, c=c: sy.dma_start(
                                r3(x1dbg)[:, :, c * CH:(c + 1) * CH],
                                xchunk.rearrange("p (d c) -> p d c", c=CH)),
                                wait=[("VE", ve_x1[0]), ("VE", ve_x1[1])],
                                inc="HQ", amt=16)
                        ln_chunk(c, xchunk, Hbig, sq, tmpf, rowbuf,
                                 [("LQ", lq), ("VE", max(ve_x1[0], ve_x1[1]))])
                        ve_h2 = P.at("VE")
                        if x1dbg is not None:
                            P.op("sync", lambda sy, c=c: sy.dma_start(
                                r3(h2d)[:, :, c * CH:(c + 1) * CH],
                                Hbig.rearrange("p (d s) -> p d s", s=S)[:, :, c * CH:(c + 1) * CH]),
                                wait=[("VE", ve_h2)], inc="HQ", amt=16)
                        for tt4 in range(4):
                            tt = 4 * c + tt4
                            half = tt % 2
                            for dblk in range(DT):
                                tq_rel[half] = P.op("scalar", lambda en, tt=tt, half=half, dblk=dblk: en.dma_start_transpose(
                                    ttileC[:, half * D + dblk * 128: half * D + (dblk + 1) * 128],
                                    Hbig.rearrange("p (d s) -> p d s", s=S)[:, dblk, 128 * tt:128 * (tt + 1)]),
                                    wait=(([("VE", ve_h2)] if dblk == 0 else []) +
                                          ([("HQ", hq_relay[half])] if dblk == 0 else [])),
                                    inc="TQ%d" % half, amt=16)
                            hq_relay[half] = P.op("sync", lambda sy, tt=tt, half=half: sy.dma_start(
                                h2tm[1 + 128 * tt:1 + 128 * (tt + 1), :],
                                ttileC[:, half * D:(half + 1) * D]),
                                wait=[("TQ%d" % half, tq_rel[half])], inc="HQ", amt=16)
                        for tt in range(4):
                            ti = 4 * c + tt
                            ve_h2fp = None
                            for d in range(DT):
                                wm2 = P.op("vector", lambda v, d=d, tt=tt: v.tensor_mul(
                                    tmpf[:, 0:128], pb4[:, tt * 128:(tt + 1) * 128],
                                    xchunk[:, d * CH + tt * 128: d * CH + (tt + 1) * 128]),
                                    wait=([("VE", ve_lgevac)] if d == 0 else [])
                                    + ([("VE", ve_h2fp)] if d else []),
                                    inc="VE")
                                ve_h2fp = P.op("vector", lambda v, d=d, tt=tt: v.tensor_add(
                                    h2fp[:, d * 128:(d + 1) * 128],
                                    pb5[:, tt * 128:(tt + 1) * 128], tmpf[:, 0:128]),
                                    wait=[("VE", wm2)], inc="VE")
                            for d in range(DT):
                                P.op("tensor", lambda t, d=d: t.matmul(
                                    pb6[:, 0:NE], h2fp[:, d * 128:(d + 1) * 128],
                                    wrf[:, d * NE:(d + 1) * NE],
                                    start=(d == 0), stop=False),
                                    wait=([("VE", ve_h2fp), ("LQ", lq_wrf)]
                                          if d == 0 else []))
                            pe_lg = P.op("tensor", lambda t: t.matmul(
                                pb6[:, 0:NE], onesrow_f[0:1, :], brf[0:1, :],
                                start=False, stop=True), inc="PE")
                            wlg = P.op("vector", lambda v: v.tensor_copy(lg_sb[:, :], pb6[:, 0:NE]),
                                       wait=[("PE", pe_lg)], inc="VE")
                            wmx = P.op("vector", lambda v: v.reduce_max(
                                statc[:, 0:1], lg_sb[:, :], axis=AX.X),
                                wait=[("VE", wlg)], inc="VE")
                            ve_lgevac = P.op("vector", lambda v: v.tensor_scalar_mul(
                                statc[:, 1:2], statc[:, 0:1], -1.0),
                                wait=[("VE", wmx)], inc="VE")
                            se_e8 = P.op("scalar", lambda sc: sc.activation(
                                e8_sb[:, :], lg_sb[:, :], AF.Exp, bias=statc[:, 1:2]),
                                wait=[("VE", ve_lgevac)], inc="SE")
                            wsm = P.op("vector", lambda v: v.reduce_sum(
                                statc[:, 2:3], e8_sb[:, :], axis=AX.X),
                                wait=[("SE", se_e8)], inc="VE")
                            ve_g = P.op("vector", lambda v: v.reciprocal(
                                statc[:, 3:4], statc[:, 2:3]),
                                wait=[("VE", wsm)], inc="VE")
                            ve_mg = P.op("vector", lambda v: v.tensor_scalar(
                                mgf[:, :], lg_sb[:, :], statc[:, 0:1], statc[:, 3:4],
                                ALU.is_equal, ALU.mult),
                                wait=[("VE", ve_g)], inc="VE")
                            pe_tp = P.op("tensor", lambda t: t.transpose(
                                pb7[0:8, 0:128], mgf[:, :], identf[:, :]),
                                wait=[("VE", ve_mg), ("VE", ve_ptbf), ("LQ", wq_ident)],
                                inc="PE")
                            ve_ptbf = P.op("vector", lambda v, ti=ti: v.tensor_copy(
                                maskgT[0:8, ti * 128:(ti + 1) * 128], pb7[0:8, 0:128]),
                                wait=[("PE", pe_tp)], inc="VE")
                        ve_maskgT_c = ve_ptbf
                        ve_ob = dict(ve_x1)
                        for d in range(DT):
                            pe_b2 = P.op("tensor", lambda t, d=d, c=c: t.matmul(
                                psO(d), b2sb[0:8, d * 128:(d + 1) * 128],
                                maskgT[0:8, c * CH:(c + 1) * CH],
                                start=True, stop=True),
                                wait=[("VE", ve_maskgT_c), ("VE", ve_ob[d % 2])],
                                inc="PE")
                            ve_ob[d % 2] = P.op("vector", lambda v, d=d: v.tensor_add(
                                xchunk[:, d * CH:(d + 1) * CH], psO(d),
                                xchunk[:, d * CH:(d + 1) * CH]),
                                wait=[("PE", pe_b2)], inc="VE")
                        ve_done = P.at("VE")
                        ve_x1 = dict(ve_ob)
                        if basedbg is not None:
                            P.op("sync", lambda sy, c=c: sy.dma_start(
                                r3(basedbg)[:, :, c * CH:(c + 1) * CH],
                                xchunk.rearrange("p (d c) -> p d c", c=CH)),
                                wait=[("VE", ve_done)], inc="HQ", amt=16)
                        oq_prev = P.op("sync", lambda sy, c=c: sy.dma_start(
                            r3(outT)[:, :, c * CH:(c + 1) * CH],
                            xchunk.rearrange("p (d c) -> p d c", c=CH)),
                            wait=[("VE", ve_done)], inc="OQ", amt=16)
                    if h2dbg is not None:
                        P.op("sync", lambda sy: sy.dma_start(
                            h2dbg[:, :], maskgT[:, :]),
                            wait=[("VE", P.at("VE"))], inc="HQ", amt=16)
                    P.flush("phaseC")
                endC = P.snap()

        SPAD = S + 1024
        NTILE_MAX = (S + 1024) // 128
        w1sb = ctx.enter_context(nc.sbuf_tensor("w1sb", [128, DT * DFF], BF16))
        w2sb = ctx.enter_context(nc.sbuf_tensor("w2sb", [128, NF * D], BF16))
        with ExitStack() as prr:
            sbR = lambda n, sh, dt: prr.enter_context(nc.sbuf_tensor(n, sh, dt))
            B01 = sbR("B01", [8, S], F32)
            CUM = sbR("CUM", [8, S], F32)
            Z8 = sbR("Z8", [8, S], F32)
            rows8 = sbR("rows8", [8, 16], F32)
            irow8 = sbR("irow8", [8, 4], I32)
            icol8 = sbR("icol8", [8, 2], I32)
            grstg = sbR("grstg", [1, CH], F32)
            pfrow = sbR("pfrow", [1, 512], I32)
            idgrid = sbR("idgrid", [128, NT], I32)
            dstcA = sbR("dstcA", [128, NT], I32)
            ltri_sb = sbR("ltri_sb", [8, 8], F32)

            P.fence(endC, engines=P.ENGINES)
            lq_lt = P.op("sync", lambda sy: sy.dma_start(ltri_sb[:, :], ltri[:, :]),
                         inc="LQ", amt=16)
            oq_base = P.at("OQ")
            pp_all = P.at("WQ")
            for ei in range(8):
                P.op("sync", lambda sy, ei=ei: sy.dma_start(
                    w1sb.rearrange("p (d f) -> p d f", f=DFF)[:, :, ei * 512:(ei + 1) * 512],
                    w1bf[0].rearrange("(d pp) f -> pp d f", pp=128)[:, :, ei * 512:(ei + 1) * 512]),
                    wait=[("WQ", pp_all)] if ei == 0 else [], inc="WS", amt=16)
                P.op("sync", lambda sy, ei=ei: sy.dma_start(
                    w2sb.rearrange("p (f m) -> p f m", m=D)[:, ei * 4:(ei + 1) * 4, :],
                    w2bf[0].rearrange("(f pp) m -> pp f m", pp=128)[:, ei * 4:(ei + 1) * 4, :]),
                    inc="WS", amt=16)

            ve_r = endC["VE"]
            hq_g = 0
            for ch in range(C):
                pe_g = P.op("tensor", lambda t, ch=ch: t.matmul(
                    pb6[0:1, 0:CH], onescol_b[0:8, 0:1],
                    maskgT[0:8, ch * CH:(ch + 1) * CH], start=True, stop=True),
                    wait=[("VE", ve_r)], inc="PE")
                ve_r = P.op("vector", lambda v: v.tensor_copy(
                    grstg[0:1, 0:CH], pb6[0:1, 0:CH]),
                    wait=[("PE", pe_g), ("HQ", hq_g)], inc="VE")
                hq_g = P.op("sync", lambda sy, ch=ch: sy.dma_start(
                    gateD[0:1, 1 + ch * CH:1 + (ch + 1) * CH], grstg[0:1, 0:CH]),
                    wait=[("VE", ve_r)], inc="HQ", amt=16)
            w = P.op("vector", lambda v: v.tensor_single_scalar(
                B01[:, :], maskgT[0:8, :], 0.0, ALU.is_gt),
                wait=[("VE", ve_r)], inc="VE")
            w = P.op("vector", lambda v: v.memset(Z8[:, :], 0.0),
                     wait=[("VE", w)], inc="VE")
            w = P.op("vector", lambda v: v.tensor_tensor_scan(
                CUM[:, :], B01[:, :], Z8[:, :], 0.0, ALU.add, ALU.add),
                wait=[("VE", w)], inc="VE")
            w = P.op("vector", lambda v: v.tensor_copy(rows8[:, 0:1], CUM[:, S - 1:S]),
                     wait=[("VE", w)], inc="VE")
            w = P.op("vector", lambda v: v.tensor_copy(irow8[:, 0:1], rows8[:, 0:1]),
                     wait=[("VE", w)], inc="VE")
            w = P.op("vector", lambda v: v.tensor_single_scalar(
                irow8[:, 1:2], irow8[:, 0:1], 127, ALU.add),
                wait=[("VE", w)], inc="VE")
            w = P.op("vector", lambda v: v.tensor_single_scalar(
                irow8[:, 2:3], irow8[:, 1:2], -128, ALU.bitwise_and),
                wait=[("VE", w)], inc="VE")
            w = P.op("vector", lambda v: v.tensor_single_scalar(
                irow8[:, 3:4], irow8[:, 2:3], 7, ALU.logical_shift_right),
                wait=[("VE", w)], inc="VE")
            w = P.op("vector", lambda v: v.tensor_copy(rows8[:, 4:5], irow8[:, 2:3]),
                     wait=[("VE", w)], inc="VE")
            pe_off = P.op("tensor", lambda t: t.matmul(
                pb7[0:8, 0:1], ltri_sb[:, :], rows8[:, 4:5],
                start=True, stop=True),
                wait=[("VE", w), ("LQ", lq_lt)], inc="PE")
            w = P.op("vector", lambda v: v.tensor_copy(rows8[:, 6:7], pb7[0:8, 0:1]),
                     wait=[("PE", pe_off)], inc="VE")
            w = P.op("vector", lambda v: v.tensor_scalar(
                CUM[:, :], CUM[:, :], rows8[:, 6:7], -1.0, ALU.add, ALU.add),
                wait=[("VE", w)], inc="VE")
            ve_dsel = P.op("vector", lambda v: v.tensor_mul(B01[:, :], CUM[:, :], B01[:, :]),
                           wait=[("VE", w)], inc="VE")
            pe_dst = None
            for tt in range(NT):
                pe_dst = P.op("tensor", lambda t, tt=tt: t.matmul(
                    pb7[:, tt:tt + 1], B01[0:8, 128 * tt:128 * (tt + 1)],
                    onescol_f[0:8, 0:1], start=True, stop=True),
                    wait=[("VE", ve_dsel)] if tt == 0 else [],
                    inc="PE" if tt == NT - 1 else None)
            ve_dst = P.op("vector", lambda v: v.tensor_copy(
                dstcA[:, 0:NT], pb7[:, 0:NT]),
                wait=[("PE", pe_dst)], inc="VE")
            w = P.op("vector", lambda v: v.tensor_copy(icol8[:, 0:1], irow8[:, 3:4]),
                     wait=[("VE", ve_dsel)], inc="VE")
            w = P.op("vector", lambda v: v.tensor_copy(icol8[:, 1:2], rows8[:, 6:7]),
                     wait=[("VE", w)], inc="VE")
            hq_int = P.op("sync", lambda sy: sy.dma_start(intd[:, :], icol8[:, :]),
                          wait=[("VE", w)], inc="HQ", amt=16)
            lq_int = P.op("sync", lambda sy: sy.dma_start(
                introw[0:1, 0:16], bass.AP(intd, 0, [[16, 1], [1, 16]])),
                wait=[("HQ", hq_int)], inc="LQ", amt=16)
            ge_pf = P.op("gpsimd", lambda g: g.memset(pfrow[:, :], 1 << 27), inc="GE")
            hq_pf = 0
            for k in range(SPAD // 512):
                hq_pf = P.op("sync", lambda sy, k=k: sy.dma_start(
                    srcd[0:1, k * 512:(k + 1) * 512], pfrow[:, :]),
                    wait=[("GE", ge_pf)] if k == 0 else [], inc="HQ", amt=16)
            wio = P.op("gpsimd", lambda g: g.iota(
                idgrid[:, :], [[128, NT]], base=1, channel_multiplier=1),
                inc="GE")
            for tt in range(NT):
                P.op("gpsimd", lambda g, tt=tt: g.indirect_dma_start(
                    srcd.rearrange("a (g o) -> (a g) o", o=1),
                    IndirectOffsetOnAxis(dstcA[:, tt:tt + 1], 0),
                    idgrid[:, tt:tt + 1], None),
                    wait=[("HQ", hq_pf), ("GE", wio), ("VE", ve_dst)] if tt == 0
                    else [],
                    inc="AQ", amt=16)
            P.flush("phaseR")
        endR = P.snap()

        with ExitStack() as pdd:
            sbD = lambda n, sh, dt: pdd.enter_context(nc.sbuf_tensor(n, sh, dt))
            h1sb = sbD("h1sb", [128, NF * 128], BF16)
            gtm = sbD("gtm", [128, 2 * D], BF16)
            gfm = sbD("gfm", [128, 2 * D], BF16)
            ytile = sbD("ytile", [128, 2 * D], BF16)
            srcc = sbD("srcc", [128, 2], I32)
            gatec = sbD("gatec", [128, 2], F32)
            b1sb = sbD("b1sb", [128, NF * NE], F32)

            P.fence(endR, engines=P.ENGINES)
            lq_b1R = P.op("sync", lambda sy: sy.dma_start(b1sb[:, :], b1l[:, :]),
                          inc="LQ", amt=16)
            lq_dyn = max(lq_int, lq_b1R)
            ge_sc = P.op("gpsimd", lambda g: g.memset(srcc[:, :], 1 << 27))
            P.op("gpsimd", lambda g: g.memset(gtm[:, :], 0.0))
            P.op("gpsimd", lambda g: g.memset(gatec[:, :], 0.0))
            ge_sc = P.op("gpsimd", lambda g: g.memset(ytile[:, :], 0.0), inc="GE")
            P.fence({"GE": P.at("GE")}, engines=("sync", "tensor", "vector", "scalar"))
            P.flush("phaseDpre")
            endR2 = P.snap()


            def fn_sync(En):
                with ExitStack() as rc:
                    git = rc.enter_context(En.register("s_git"))
                    t1 = rc.enter_context(En.register("s_t1"))
                    pr_ = rc.enter_context(En.register("s_pr"))
                    lagP = rc.enter_context(En.register("s_lagP"))
                    nr = rc.enter_context(En.register("s_nr"))
                    En.reg_mov(git, 0)
                    En.reg_mov(lagP, 0)
                    for e in range(NE):
                        if e > 0:
                            En.wait_ge(P.h["EBPE"], e)
                            for ei in range(8):
                                En.dma_start(
                                    w1sb.rearrange("p (d f) -> p d f", f=DFF)[:, :, ei * 512:(ei + 1) * 512],
                                    w1bf[e].rearrange("(d pp) f -> pp d f", pp=128)[:, :, ei * 512:(ei + 1) * 512],
                                ).then_inc(P.h["WS"], 16)
                                En.dma_start(
                                    w2sb.rearrange("p (f m) -> p f m", m=D)[:, ei * 4:(ei + 1) * 4, :],
                                    w2bf[e].rearrange("(f pp) m -> pp f m", pp=128)[:, ei * 4:(ei + 1) * 4, :],
                                ).then_inc(P.h["WS"], 16)
                        En.wait_ge(P.h["LQ"], lq_dyn)
                        En.reg_load(nr, introw[0:1, 2 * e:2 * e + 1])
                        with En.Fori(0, nr) as it:
                            En.reg_div(t1, git, 2)
                            En.reg_add(t1, t1, 1)
                            En.reg_mul(t1, t1, 16)
                            En.wait_ge(P.h["DPE"], lagP)
                            En.reg_mod(pr_, git, 2)
                            with En.If_eq(pr_, 0):
                                En.wait_ge(P.h["GG0"], t1)
                                for dblk in range(DT):
                                    En.dma_start_transpose(
                                        gfm[:, dblk * 128:(dblk + 1) * 128],
                                        gtm[:, dblk * 128:(dblk + 1) * 128],
                                    ).then_inc(P.h["TQ"], 16)
                            with En.Else():
                                En.wait_ge(P.h["GG1"], t1)
                                for dblk in range(DT):
                                    En.dma_start_transpose(
                                        gfm[:, D + dblk * 128:D + (dblk + 1) * 128],
                                        gtm[:, D + dblk * 128:D + (dblk + 1) * 128],
                                    ).then_inc(P.h["TQ"], 16)
                            En.reg_mul(lagP, git, 33)
                            En.reg_add(git, git, 1)
                        En.sem_inc(P.h["EBSY"], 1)

            def fn_gpsimd(En):
                with ExitStack() as rc:
                    git = rc.enter_context(En.register("g_git"))
                    t1 = rc.enter_context(En.register("g_t1"))
                    pr_ = rc.enter_context(En.register("g_pr"))
                    nr = rc.enter_context(En.register("g_nr"))
                    offc = rc.enter_context(En.register("g_off"))
                    bc_r = rc.enter_context(En.register("g_bc"))
                    En.reg_mov(bc_r, S)
                    En.reg_mov(git, 0)
                    En.wait_ge(P.h["HQ"], endR["HQ"])
                    En.wait_ge(P.h["AQ"], endR["AQ"])
                    En.wait_ge(P.h["LQ"], lq_dyn)

                    def tilebody(par):
                        En.reg_add(t1, git, 1)
                        En.reg_div(t1, t1, 2)
                        En.reg_mul(t1, t1, 16)
                        En.wait_ge(P.h["GY%d" % par], t1)
                        En.dma_start(
                            srcc[:, par:par + 1],
                            bass.AP(srcd, offc, [[1, 128], [1, 1]]),
                        ).then_inc(P.h["GS"], 16)
                        En.reg_mul(t1, git, 16)
                        En.reg_add(t1, t1, 16)
                        En.wait_ge(P.h["GS"], t1)
                        En.reg_mul(t1, git, 128)
                        En.wait_ge(P.h["TQ"], t1)
                        En.indirect_dma_start(
                            gtm[:, par * D:(par + 1) * D], None,
                            h2tm[:, :],
                            IndirectOffsetOnAxis(srcc[:, par:par + 1], 0),
                            bounds_check=bc_r, oob_is_err=False,
                        ).then_inc(P.h["GG%d" % par], 16)
                        En.indirect_dma_start(
                            gatec[:, par:par + 1], None,
                            gateD.rearrange("a (g o) -> (a g) o", o=1),
                            IndirectOffsetOnAxis(srcc[:, par:par + 1], 0),
                            bounds_check=bc_r, oob_is_err=False,
                        ).then_inc(P.h["GC%d" % par], 16)
                        En.wait_ge(P.h["DVE"], git)
                        En.indirect_dma_start(
                            ytm[:, :],
                            IndirectOffsetOnAxis(srcc[:, 1 - par:2 - par], 0),
                            ytile[:, (1 - par) * D:(2 - par) * D], None,
                            bounds_check=bc_r, oob_is_err=False,
                        ).then_inc(P.h["GY%d" % (1 - par)], 16)

                    En.reg_load(nr, introw[0:1, 0:1])
                    for e in range(1, NE):
                        En.reg_load(t1, introw[0:1, 2 * e:2 * e + 1])
                        En.reg_add(nr, nr, t1)
                    En.reg_mov(offc, 0)
                    with En.Fori(0, nr) as it:
                        En.reg_mod(pr_, git, 2)
                        with En.If_eq(pr_, 0):
                            tilebody(0)
                        with En.Else():
                            tilebody(1)
                        En.reg_add(git, git, 1)
                        En.reg_add(offc, offc, 128)
                    En.sem_inc(P.h["EBGQ"], NE)
                    En.wait_ge(P.h["DVE"], git)
                    En.reg_mod(pr_, git, 2)
                    with En.If_eq(pr_, 0):
                        En.indirect_dma_start(
                            ytm[:, :], IndirectOffsetOnAxis(srcc[:, 1:2], 0),
                            ytile[:, D:2 * D], None,
                            bounds_check=bc_r, oob_is_err=False,
                        ).then_inc(P.h["GY1"], 16)
                    with En.Else():
                        En.indirect_dma_start(
                            ytm[:, :], IndirectOffsetOnAxis(srcc[:, 0:1], 0),
                            ytile[:, 0:D], None,
                            bounds_check=bc_r, oob_is_err=False,
                        ).then_inc(P.h["GY0"], 16)
                    En.reg_add(offc, git, 1)
                    En.reg_div(offc, offc, 2)
                    En.reg_mul(t1, offc, 16)
                    En.wait_ge(P.h["GY0"], t1)
                    En.reg_add(offc, offc, 1)
                    En.reg_sub(offc, offc, pr_)
                    En.reg_mul(t1, offc, 16)
                    En.wait_ge(P.h["GY1"], t1)
                    En.sem_inc(P.h["SDONE"], 1)

            def fn_pe(En):
                with ExitStack() as rc:
                    git = rc.enter_context(En.register("p_git"))
                    t1 = rc.enter_context(En.register("p_t1"))
                    t2 = rc.enter_context(En.register("p_t2"))
                    pr_ = rc.enter_context(En.register("p_pr"))
                    nr = rc.enter_context(En.register("p_nr"))
                    En.reg_mov(git, 0)
                    En.wait_ge(P.h["LQ"], lq_dyn)

                    def h1mm(f, par):
                        if f >= 2:
                            En.reg_add(t1, t2, f - 1)
                            En.wait_ge(P.h["DSE"], t1)
                        for d in range(DT):
                            mm = En.matmul(
                                pw0[:, (f % 2) * 512:(f % 2) * 512 + 128],
                                w1sb[:, d * DFF + f * 128:d * DFF + (f + 1) * 128],
                                gfm[:, par * D + d * 128:par * D + (d + 1) * 128],
                                start=(d == 0), stop=(d == DT - 1),
                            )
                            if d == DT - 1:
                                mm.then_inc(P.h["DPE"], 1)

                    def yemm(f):
                        En.reg_add(t1, t2, f + 1)
                        En.wait_ge(P.h["DSE"], t1)
                        for h in range(2):
                            mm = En.matmul(
                                (pb4, pb5)[h][:, 0:CH],
                                h1sb[:, f * 128:(f + 1) * 128],
                                w2sb[:, f * D + h * 512:f * D + (h + 1) * 512],
                                start=(f == 0), stop=(f == NF - 1),
                            )
                            if f == NF - 1 and h == 1:
                                mm.then_inc(P.h["DPE"], 1)

                    def tilecompute(par):
                        h1mm(0, par)
                        for f in range(1, NF):
                            h1mm(f, par)
                            yemm(f - 1)
                        yemm(NF - 1)

                    for e in range(NE):
                        En.wait_ge(P.h["WS"], 256 * (e + 1))
                        En.reg_load(nr, introw[0:1, 2 * e:2 * e + 1])
                        with En.Fori(0, nr) as it:
                            En.reg_mul(t1, git, 128)
                            En.reg_add(t1, t1, 128)
                            En.wait_ge(P.h["TQ"], t1)
                            En.wait_ge(P.h["DVE"], git)
                            En.reg_mul(t2, git, 32)
                            En.reg_mod(pr_, git, 2)
                            with En.If_eq(pr_, 0):
                                tilecompute(0)
                            with En.Else():
                                tilecompute(1)
                            En.reg_add(git, git, 1)
                        En.sem_inc(P.h["EBPE"], 1)

            def fn_scalar(En):
                with ExitStack() as rc:
                    git = rc.enter_context(En.register("c_git"))
                    t1 = rc.enter_context(En.register("c_t1"))
                    t2 = rc.enter_context(En.register("c_t2"))
                    nr = rc.enter_context(En.register("c_nr"))
                    En.reg_mov(git, 0)
                    En.wait_ge(P.h["LQ"], lq_dyn)
                    for e in range(NE):
                        En.reg_load(nr, introw[0:1, 2 * e:2 * e + 1])
                        with En.Fori(0, nr) as it:
                            En.reg_mul(t2, git, 33)
                            for f in range(NF):
                                En.reg_add(t1, t2, f + 1)
                                En.wait_ge(P.h["DPE"], t1)
                                En.activation(
                                    h1sb[:, f * 128:(f + 1) * 128],
                                    pw0[:, (f % 2) * 512:(f % 2) * 512 + 128],
                                    AF.Relu,
                                    bias=b1sb[:, f * NE + e:f * NE + e + 1],
                                ).then_inc(P.h["DSE"], 1)
                            En.reg_add(git, git, 1)
                        En.sem_inc(P.h["EBSE"], 1)

            def fn_vector(En):
                with ExitStack() as rc:
                    git = rc.enter_context(En.register("v_git"))
                    t1 = rc.enter_context(En.register("v_t1"))
                    pr_ = rc.enter_context(En.register("v_pr"))
                    lagS = rc.enter_context(En.register("v_lagS"))
                    nr = rc.enter_context(En.register("v_nr"))
                    En.reg_mov(git, 0)
                    En.reg_mov(lagS, 0)
                    En.wait_ge(P.h["LQ"], lq_dyn)

                    def evac(par):
                        for h in range(2):
                            op = En.tensor_scalar(
                                ytile[:, par * D + h * 512:par * D + (h + 1) * 512],
                                (pb4, pb5)[h][:, 0:CH],
                                gatec[:, par:par + 1], None, ALU.mult,
                            )
                            if h == 1:
                                op.then_inc(P.h["DVE"], 1)

                    for e in range(NE):
                        En.reg_load(nr, introw[0:1, 2 * e:2 * e + 1])
                        with En.Fori(0, nr) as it:
                            En.reg_mul(t1, git, 33)
                            En.reg_add(t1, t1, 33)
                            En.wait_ge(P.h["DPE"], t1)
                            En.reg_div(t1, git, 2)
                            En.reg_add(t1, t1, 1)
                            En.reg_mul(t1, t1, 16)
                            En.reg_add(lagS, git, 1)
                            En.reg_div(lagS, lagS, 2)
                            En.reg_mul(lagS, lagS, 16)
                            En.reg_mod(pr_, git, 2)
                            with En.If_eq(pr_, 0):
                                En.wait_ge(P.h["GC0"], t1)
                                En.wait_ge(P.h["GY0"], lagS)
                                evac(0)
                            with En.Else():
                                En.wait_ge(P.h["GC1"], t1)
                                En.wait_ge(P.h["GY1"], lagS)
                                evac(1)
                            En.reg_add(git, git, 1)
                        En.sem_inc(P.h["EBVE"], 1)

            with nc.Block("phaseDdyn") as dblk:
                dblk.gpsimd(fn_gpsimd)
                dblk.sync(fn_sync)
                dblk.tensor(fn_pe)
                dblk.scalar(fn_scalar)
                dblk.vector(fn_vector)
            P.cnt["EBPE"] += NE
            P.cnt["EBSY"] += NE
            P.cnt["EBGQ"] += NE
            P.cnt["EBSE"] += NE
            P.cnt["EBVE"] += NE
            P.cnt["SDONE"] += 1
            P.cnt["WS"] += 256 * (NE - 1)

            if debug_x1:
                dbg_ye = nc.dram_tensor("dbg_ye", [128, 2 * CH], F32,
                                        kind="ExternalOutput")
                dbg_yt = nc.dram_tensor("dbg_yt", [128, 2 * D], BF16,
                                        kind="ExternalOutput")
                dbg_gfm = nc.dram_tensor("dbg_gfm", [128, 2 * D], BF16,
                                         kind="ExternalOutput")
                dbg_h1 = nc.dram_tensor("dbg_h1", [128, NF * 128], BF16,
                                        kind="ExternalOutput")
                dbg_w1 = nc.dram_tensor("dbg_w1", [128, DT * DFF], BF16,
                                        kind="ExternalOutput")
                dbg_w2 = nc.dram_tensor("dbg_w2", [128, NF * D], BF16,
                                        kind="ExternalOutput")
                yecp = pdd.enter_context(nc.sbuf_tensor("yecp", [128, 2 * CH], F32))
                P.waitonly("vector", [("SDONE", 1)])
                P.waitonly("sync", [("SDONE", 1)])
                ve_ye = P.op("vector", lambda v: v.tensor_copy(
                    yecp[:, 0:CH], pb4[:, 0:CH]), inc="VE")
                ve_ye = P.op("vector", lambda v: v.tensor_copy(
                    yecp[:, CH:2 * CH], pb5[:, 0:CH]),
                    wait=[("VE", ve_ye)], inc="VE")
                P.op("sync", lambda sy: sy.dma_start(dbg_ye[:, :], yecp[:, :]),
                     wait=[("VE", ve_ye)], inc="HQ", amt=16)
                P.op("sync", lambda sy: sy.dma_start(dbg_yt[:, :], ytile[:, :]),
                     inc="HQ", amt=16)
                P.op("sync", lambda sy: sy.dma_start(dbg_gfm[:, :], gfm[:, :]),
                     inc="HQ", amt=16)
                P.op("sync", lambda sy: sy.dma_start(dbg_h1[:, :], h1sb[:, :]),
                     inc="HQ", amt=16)
                P.op("sync", lambda sy: sy.dma_start(dbg_w1[:, :], w1sb[:, :]),
                     inc="HQ", amt=16)
                P.op("sync", lambda sy: sy.dma_start(dbg_w2[:, :], w2sb[:, :]),
                     inc="HQ", amt=16)
                P.flush("dbgdump")

            with ExitStack() as pe_:
                sbE = lambda n, sh, dt: pe_.enter_context(nc.sbuf_tensor(n, sh, dt))
                yfm = sbE("yfmE", [128, 2 * 4096], BF16)
                P.waitonly("sync", [("SDONE", 1)])
                P.waitonly("gpsimd", [("SDONE", 1)])
                aq_e = {0: 0, 1: 0}
                tq_e = {0: 0, 1: 0}
                NG = S // 512
                for g in range(NG):
                    half = g % 2
                    for dblk2 in range(DT):
                        tq_e[half] = P.op("sync", lambda sy, g=g, half=half, dblk2=dblk2: sy.dma_start_transpose(
                            yfm[:, half * 4096 + dblk2 * 512: half * 4096 + (dblk2 + 1) * 512],
                            ytm[1 + 512 * g:1 + 512 * (g + 1), dblk2 * 128:(dblk2 + 1) * 128]),
                            wait=(([("AQ%d" % half, aq_e[half])] if dblk2 == 0 else []) +
                                  ([("HQ", P.at("HQ"))] if g < 2 and dblk2 == 0
                                   else [])),
                            inc="TQ%d" % half, amt=16)
                    aq_e[half] = P.op("gpsimd", lambda gp, g=g, half=half: gp.dma_start(
                        r3(outT)[:, :, 512 * g:512 * (g + 1)],
                        yfm[:, half * 4096:(half + 1) * 4096].rearrange("p (d c) -> p d c", c=512),
                        accum_op=ALU.add),
                        wait=[("TQ%d" % half, tq_e[half]), ("OQ", oq_base)],
                        inc="AQ%d" % half, amt=16)
                P.waitonly("sync", [("OQ", P.at("OQ")), ("AQ0", P.at("AQ0")),
                                    ("AQ1", P.at("AQ1")), ("AQ", P.at("AQ")),
                                    ("LQ", P.at("LQ")), ("HQ", P.at("HQ"))])
                P.flush("phaseE")

    return nc


def _consts(S):
    t = np.arange(S, dtype=np.float32)
    i2 = np.arange(0, DH, 2, dtype=np.float32)
    scale = (1.0 / (10000.0 ** (i2 / DH) + np.float32(np.finfo(np.float32).eps))).astype(np.float32)
    ang = (t[:, None] * scale[None, :]).astype(np.float32)
    pos = np.stack([np.sin(ang), np.cos(ang)], axis=-1).reshape(S, DH).astype(np.float32)
    pos2 = np.concatenate([np.ascontiguousarray(pos.T)] * 2, axis=0)
    kk = np.arange(128)[:, None]
    qq = np.arange(CH)[None, :]
    masks = np.concatenate(
        [np.where(128 * r + kk <= qq, 0.0, NEG).astype(np.float32) for r in range(4)],
        axis=1)
    ident = np.eye(128, dtype=np.float32)
    return np.ascontiguousarray(pos2), np.ascontiguousarray(masks), ident


def kernel(L0_attn_x, L0_attn_Wq, L0_attn_Wk, L0_attn_Wv, L0_attn_Wo,
           L0_ffn_Wr, L0_ffn_br, L0_ffn_W1, L0_ffn_b1, L0_ffn_W2, L0_ffn_b2):
    x = np.asarray(L0_attn_x, dtype=np.float32)
    B, S, _ = x.shape
    pos2, masks, ident = _consts(S)
    b1 = np.asarray(L0_ffn_b1, dtype=np.float32)
    b1l = np.ascontiguousarray(
        b1.reshape(NE, NF, 128).transpose(2, 1, 0).reshape(128, NF * NE))
    shared = {
        "wq": np.ascontiguousarray(L0_attn_Wq, dtype=np.float32),
        "wk": np.ascontiguousarray(L0_attn_Wk, dtype=np.float32),
        "wv": np.ascontiguousarray(L0_attn_Wv, dtype=np.float32),
        "wo": np.ascontiguousarray(L0_attn_Wo, dtype=np.float32),
        "wr": np.ascontiguousarray(L0_ffn_Wr, dtype=np.float32),
        "brr": np.ascontiguousarray(
            np.asarray(L0_ffn_br, dtype=np.float32).reshape(1, NE)),
        "w1": np.ascontiguousarray(L0_ffn_W1, dtype=np.float32),
        "b1l": b1l,
        "w2": np.ascontiguousarray(L0_ffn_W2, dtype=np.float32),
        "b2": np.ascontiguousarray(L0_ffn_b2, dtype=np.float32),
        "pos2": pos2, "masksd": masks, "identd": ident,
        "ltri": np.ascontiguousarray(np.triu(np.ones((8, 8), dtype=np.float32), 1)),
    }
    nc = build(S)
    in_maps = []
    for i in range(B):
        m = dict(shared)
        m["xT"] = np.ascontiguousarray(x[i].T)
        in_maps.append(m)
    res = run_bass_kernel_spmd(nc, in_maps, core_ids=list(range(8)))
    out = np.stack([np.ascontiguousarray(np.asarray(r["outT"]).T)
                    for r in res.results], axis=0)
    return out.astype(np.float32)


# version 14 (final)
# speedup vs baseline: 1.0229x; 1.0229x over previous
import numpy as np
import concourse.bass as bass
import concourse.mybir as mybir
from concourse.bass_utils import run_bass_kernel_spmd
from concourse.bass_core_rust import IndirectOffsetOnAxis
from contextlib import ExitStack

F32 = mybir.dt.float32
I32 = mybir.dt.int32
BF16 = mybir.dt.bfloat16
AF = mybir.ActivationFunctionType
ALU = mybir.AluOpType
AX = mybir.AxisListType

D = 1024
DT = 8
NP = 8
NE = 8
DFF = 4096
NF = 32
CH = 512
DH = 64
EPS = 1e-5
NEG = -1e30
SCALE = 1024.0 ** -0.5


class Plan:

    ENGINES = ("tensor", "vector", "scalar", "gpsimd", "sync")

    def __init__(self, nc, ctx):
        self.nc = nc
        self.pend = {e: [] for e in self.ENGINES}
        self.cnt = {}
        self.h = {}
        for name in (["LQ", "HQ", "OQ", "AQ", "WQ", "PE", "VE", "SE", "GE",
                      "GQ2", "TQ", "DPE", "DSE", "DVE", "EBPE", "EBGQ", "EBSY",
                      "EBSE", "EBVE", "SDONE", "WS",
                      "GS", "GG0", "GG1", "GC0", "GC1", "GY0", "GY1",
                      "AQ0", "AQ1", "TQ0", "TQ1"]):
            self.h[name] = ctx.enter_context(nc.semaphore(name))
            self.cnt[name] = 0

    def op(self, eng, fn, wait=(), inc=None, amt=1):
        w = [(s, v) for (s, v) in wait if v > 0]
        for s, v in w:
            assert v <= self.cnt[s], f"wait on future {s}:{v}>{self.cnt[s]}"
        after = None
        if inc is not None:
            self.cnt[inc] += amt
            after = self.cnt[inc]

        def run(En, w=w, fn=fn, inc=inc, amt=amt):
            for s, v in w:
                En.wait_ge(self.h[s], v)
            r = fn(En)
            if inc is not None:
                r.then_inc(self.h[inc], amt)

        self.pend[eng].append(run)
        return after

    def waitonly(self, eng, wait):
        w = [(s, v) for (s, v) in wait if v > 0]
        if not w:
            return
        for s, v in w:
            assert v <= self.cnt[s]

        def run(En, w=w):
            for s, v in w:
                En.wait_ge(self.h[s], v)

        self.pend[eng].append(run)

    def at(self, s):
        return self.cnt[s]

    def snap(self):
        return dict(self.cnt)

    def fence(self, snap, engines=ENGINES):
        for e in engines:
            self.waitonly(e, list(snap.items()))

    def flush(self, name=None):
        with self.nc.Block(name) as b:
            for eng in self.ENGINES:
                ops = self.pend[eng]
                if ops:
                    getattr(b, eng)(lambda En, ops=ops: [f(En) for f in ops])
        self.pend = {e: [] for e in self.ENGINES}


def build(S=4096, debug_x1=False):
    assert S % 512 == 0
    C = S // CH
    NT = S // 128
    nc = bass.Bass("TRN2", target_bir_lowering=False)
    try:
        nc.disable_value_cache = True
    except Exception:
        pass
    x1dbg = (nc.dram_tensor("x1dbg", [D, S], F32, kind="ExternalOutput")
             if debug_x1 else None)
    basedbg = (nc.dram_tensor("basedbg", [D, S], F32, kind="ExternalOutput")
               if debug_x1 else None)
    oadbg = (nc.dram_tensor("oadbg", [128, NP * S], BF16, kind="ExternalOutput")
             if debug_x1 else None)
    h2dbg = (nc.dram_tensor("h2dbg", [8, S], BF16, kind="ExternalOutput")
             if debug_x1 else None)

    xT = nc.dram_tensor("xT", [D, S], F32, kind="ExternalInput")
    wq = nc.dram_tensor("wq", [D, D], F32, kind="ExternalInput")
    wk = nc.dram_tensor("wk", [D, D], F32, kind="ExternalInput")
    wv = nc.dram_tensor("wv", [D, D], F32, kind="ExternalInput")
    wo = nc.dram_tensor("wo", [D, D], F32, kind="ExternalInput")
    wr = nc.dram_tensor("wr", [D, NE], F32, kind="ExternalInput")
    brr = nc.dram_tensor("brr", [1, NE], F32, kind="ExternalInput")
    w1 = nc.dram_tensor("w1", [NE, D, DFF], F32, kind="ExternalInput")
    b1l = nc.dram_tensor("b1l", [128, NF * NE], F32, kind="ExternalInput")
    w2 = nc.dram_tensor("w2", [NE, DFF, D], F32, kind="ExternalInput")
    b2 = nc.dram_tensor("b2", [NE, D], F32, kind="ExternalInput")
    pos2 = nc.dram_tensor("pos2", [128, S], F32, kind="ExternalInput")
    masksd = nc.dram_tensor("masksd", [128, 4 * CH], F32, kind="ExternalInput")
    identd = nc.dram_tensor("identd", [128, 128], F32, kind="ExternalInput")
    h2d = nc.dram_tensor("h2d", [D, S], BF16,
                         **({"kind": "ExternalOutput"} if debug_x1 else {}))
    w1bf = nc.dram_tensor("w1bf", [NE, D, DFF], BF16)
    w2bf = nc.dram_tensor("w2bf", [NE, DFF, D], BF16)
    dbgkind = {"kind": "ExternalOutput"} if debug_x1 else {}
    h2tm = nc.dram_tensor("h2tm", [S + 1, D], BF16, **dbgkind)
    ytm = nc.dram_tensor("ytm", [S + 1, D], BF16, **dbgkind)
    srcd = nc.dram_tensor("srcd", [1, S + 1024], mybir.dt.int32, **dbgkind)
    gateD = nc.dram_tensor("gateD", [1, S + 1], F32, **dbgkind)
    intd = nc.dram_tensor("intd", [8, 2], mybir.dt.int32,
                          **({"kind": "ExternalOutput"} if debug_x1 else {}))
    ltri = nc.dram_tensor("ltri", [8, 8], F32, kind="ExternalInput")
    outT = nc.dram_tensor("outT", [D, S], F32, kind="ExternalOutput")

    r3 = lambda t: t.rearrange("(dt pp) s -> pp dt s", pp=128)

    with ExitStack() as ctx:
        P = Plan(nc, ctx)
        sb = lambda n, sh, dt: ctx.enter_context(nc.sbuf_tensor(n, sh, dt))

        onesrow_f = sb("onesrow_f", [1, 128], F32)
        onesrow_b = sb("onesrow_b", [1, 128], BF16)
        onescol_f = sb("onescol_f", [128, 1], F32)
        onescol_b = sb("onescol_b", [128, 1], BF16)
        ident = sb("ident", [128, 128], BF16)
        identf = sb("identf", [128, 128], F32)
        maskgT = sb("maskgT", [8, S], BF16)
        introw = sb("introw", [1, 32], I32)
        pw0 = ctx.enter_context(nc.psum_tensor("pw0", [128, 1024], F32))
        pw1 = ctx.enter_context(nc.psum_tensor("pw1", [128, 1024], F32))
        pb4 = ctx.enter_context(nc.psum_tensor("pb4", [128, 512], F32))
        pb5 = ctx.enter_context(nc.psum_tensor("pb5", [128, 512], F32))
        pb6 = ctx.enter_context(nc.psum_tensor("pb6", [128, 512], F32))
        pb7 = ctx.enter_context(nc.psum_tensor("pb7", [128, 512], F32))

        P.op("gpsimd", lambda g: g.memset(onesrow_f[:, :], 1.0), inc="GE")
        P.op("gpsimd", lambda g: g.memset(onesrow_b[:, :], 1.0), inc="GE")
        P.op("gpsimd", lambda g: g.memset(onescol_f[:, :], 1.0), inc="GE")
        ge_ones = P.op("gpsimd", lambda g: g.memset(onescol_b[:, :], 1.0), inc="GE")
        P.op("gpsimd", lambda g: g.dma_start(ident[:, :], identd[:, :]),
             inc="WQ", amt=16)
        wq_ident = P.op("sync", lambda sy: sy.dma_start(identf[:, :], identd[:, :]),
                        inc="LQ", amt=16)

        st = {"ve_rows": 0, "pe_sq": [0, 0]}

        def ln_chunk(c, xchunk, Hbig, sq, tmpf, rowbuf, ready):
            for d in range(DT):
                sqs = sq[:, (d % 2) * CH:(d % 2 + 1) * CH]
                xs = xchunk[:, d * CH:(d + 1) * CH]
                se_sq = P.op("scalar", lambda sc, sqs=sqs, xs=xs: sc.activation(
                    sqs, xs, AF.Square),
                    wait=(list(ready) if d == 0 else []) + [("PE", st["pe_sq"][d % 2])],
                    inc="SE")
                w0 = (list(ready) + [("GE", ge_ones), ("VE", st["ve_rows"])]
                      if d == 0 else [])
                P.op("tensor", lambda t, xs=xs, d=d: t.matmul(
                    pw1[0:1, 0:CH], onescol_f[:, :], xs,
                    start=(d == 0), stop=(d == DT - 1)), wait=w0)
                st["pe_sq"][d % 2] = P.op("tensor", lambda t, sqs=sqs, d=d: t.matmul(
                    pw1[0:1, 512:512 + CH], onescol_f[:, :], sqs,
                    start=(d == 0), stop=(d == DT - 1)),
                    wait=[("SE", se_sq)], inc="PE")
            pe_stats = P.at("PE")
            ME, MS, M2, VA, SD, RS, RM, R0 = (rowbuf[0:1, i * 512:(i + 1) * 512]
                                              for i in range(8))
            w = P.op("vector", lambda v: v.tensor_scalar_mul(ME, pw1[0:1, 0:CH], 1.0 / D),
                     wait=[("PE", pe_stats)], inc="VE")
            w = P.op("vector", lambda v: v.tensor_scalar_mul(MS, pw1[0:1, 512:512 + CH], 1.0 / D),
                     wait=[("VE", w)], inc="VE")
            w = P.op("vector", lambda v: v.tensor_mul(M2, ME, ME),
                     wait=[("VE", w)], inc="VE")
            ve_var = P.op("vector", lambda v: v.scalar_tensor_tensor(
                VA, MS, EPS, M2, ALU.add, ALU.subtract), wait=[("VE", w)], inc="VE")
            se_sd = P.op("scalar", lambda sc: sc.activation(SD, VA, AF.Sqrt),
                         wait=[("VE", ve_var)], inc="SE")
            w = P.op("vector", lambda v: v.reciprocal(R0, SD),
                     wait=[("SE", se_sd)], inc="VE")
            w = P.op("vector", lambda v: v.tensor_mul(M2, SD, R0),
                     wait=[("VE", w)], inc="VE")
            w = P.op("vector", lambda v: v.tensor_scalar(M2, M2, -1.0, 2.0,
                                                         ALU.mult, ALU.add),
                     wait=[("VE", w)], inc="VE")
            w = P.op("vector", lambda v: v.tensor_mul(RS, R0, M2),
                     wait=[("VE", w)], inc="VE")
            st["ve_rows"] = P.op("vector", lambda v: v.scalar_tensor_tensor(
                RM, ME, -1.0, RS, ALU.mult, ALU.mult), wait=[("VE", w)], inc="VE")
            P.op("tensor", lambda t: t.matmul(pb4[:, 0:CH], onesrow_f[0:1, :],
                                              RS, start=True, stop=True),
                 wait=[("VE", st["ve_rows"])])
            pe_bc = P.op("tensor", lambda t: t.matmul(pb5[:, 0:CH], onesrow_f[0:1, :],
                                                      RM, start=True, stop=True),
                         inc="PE")
            ve_h = 0
            for d in range(DT):
                xs = xchunk[:, d * CH:(d + 1) * CH]
                wm = P.op("vector", lambda v, xs=xs: v.tensor_mul(
                    tmpf[:, :], pb4[:, 0:CH], xs),
                    wait=([("PE", pe_bc)] if d == 0 else []) + ([("VE", ve_h)] if d else []),
                    inc="VE")
                ve_h = P.op("vector", lambda v, d=d: v.tensor_add(
                    Hbig[:, d * S + c * CH: d * S + (c + 1) * CH],
                    pb5[:, 0:CH], tmpf[:, :]), wait=[("VE", wm)], inc="VE")
            return ve_h

        with ExitStack() as abc:
            sbH = lambda n, sh, dt: abc.enter_context(nc.sbuf_tensor(n, sh, dt))
            Hbig = sbH("Hbig", [128, DT * S], BF16)
            rowbuf = sbH("rowbuf", [1, 4096], F32)
            Hs = lambda d, a, b: Hbig[:, d * S + a: d * S + b]

            with ExitStack() as pa:
                sbA = lambda n, sh, dt: pa.enter_context(nc.sbuf_tensor(n, sh, dt))
                xchunk = sbA("xchunkA", [128, DT * CH], F32)
                sq = sbA("sqA", [128, 2 * CH], F32)
                tmpf = sbA("tmpfA", [128, CH], F32)
                ve_h = 0
                for c in range(C):
                    lq = P.op("sync", lambda sy, c=c: sy.dma_start(
                        xchunk.rearrange("p (d c) -> p d c", c=CH),
                        r3(xT)[:, :, c * CH:(c + 1) * CH]),
                        wait=[("VE", ve_h)], inc="LQ", amt=16)
                    ve_h = ln_chunk(c, xchunk, Hbig, sq, tmpf, rowbuf, [("LQ", lq)])
                P.flush("phaseA")
            endA = P.snap()

            with ExitStack() as bcs:
                sbBC = lambda n, sh, dt: bcs.enter_context(nc.sbuf_tensor(n, sh, dt))
                out_all = sbBC("out_all", [128, NP * S], BF16)

                with ExitStack() as pbs:
                    sbB = lambda n, sh, dt: pbs.enter_context(nc.sbuf_tensor(n, sh, dt))
                    QP = sbB("QP", [128, S], BF16)
                    KP = sbB("KP", [128, S], BF16)
                    VP = sbB("VP", [128, NT * 130], BF16)
                    wqp = sbB("wqp", [128, D], BF16)
                    wkp = sbB("wkp", [128, D], BF16)
                    wvp = sbB("wvp", [128, D], BF16)
                    U = sbB("U", [128, 4 * 1024], BF16)
                    pos2b = sbB("pos2b", [128, S], BF16)
                    masksb = sbB("masksb", [128, 4 * CH], BF16)
                    bc_sb = sbB("bc_sb", [64, 2 * CH], F32)
                    rrb = rowbuf[0:1, 0:CH].bitcast(BF16)

                    P.fence(endA, engines=("gpsimd",))
                    P.op("gpsimd", lambda g: g.dma_start(pos2b[:, :], pos2[:, :]),
                         inc="WQ", amt=16)
                    wq_pm = P.op("gpsimd", lambda g: g.dma_start(masksb[:, :], masksd[:, :]),
                                 inc="WQ", amt=16)
                    ge_vones = P.op("gpsimd", lambda g: g.memset(
                        VP.rearrange("p (t w) -> p t w", w=130)[:, :, 64:65], 1.0),
                        inc="GE")
                    ge_vones = P.op("gpsimd", lambda g: g.memset(
                        VP.rearrange("p (t w) -> p t w", w=130)[:, :, 129:130], 1.0),
                        inc="GE")
                    P.fence(endA, engines=("tensor", "vector", "scalar", "sync"))

                    pe_lastuse_w = endA["PE"]
                    widefree_s = [("VE", endA["VE"]), ("VE", endA["VE"])]
                    pe_uslot_s = [[endA["PE"]] * 2, [endA["PE"]] * 2]
                    pe_bcr_s = [endA["PE"], endA["PE"]]
                    ve_outall_s = [endA["VE"], endA["VE"]]
                    ve_evacq = endA["VE"]
                    ve_evack = endA["VE"]
                    ve_pb67 = {0: endA["VE"], 1: endA["VE"]}
                    for p in range(NP):
                        wq_w = None
                        for (dst, src) in ((wqp, wq), (wkp, wk), (wvp, wv)):
                            wq_w = P.op("gpsimd", lambda g, dst=dst, src=src, p=p: g.dma_start(
                                dst.rearrange("pp (d m) -> pp d m", m=128),
                                src.rearrange("(d pp) m -> pp d m", pp=128)[:, :, p * 128:(p + 1) * 128]),
                                wait=[("PE", pe_lastuse_w)], inc="WQ", amt=16)
                        P.op("gpsimd", lambda g, p=p: g.dma_start(w1bf[p], w1[p]),
                             inc="WQ", amt=16)
                        P.op("gpsimd", lambda g, p=p: g.dma_start(w2bf[p], w2[p]),
                             inc="WQ", amt=16)
                        for c in range(C):
                            peq = None
                            for d in range(DT):
                                peq = P.op("tensor", lambda t, d=d, c=c: t.matmul(
                                    pw0[:, 0:CH], wqp[:, d * 128:(d + 1) * 128],
                                    Hs(d, c * CH, (c + 1) * CH),
                                    start=(d == 0), stop=(d == DT - 1)),
                                    wait=([("WQ", wq_w), widefree_s[0],
                                           ("VE", ve_evacq)] if d == 0 else []),
                                    inc="PE" if d == DT - 1 else None)
                            ve_evacq = P.op("vector", lambda v, c=c: v.tensor_add(
                                QP[:, c * CH:(c + 1) * CH], pw0[:, 0:CH],
                                pos2b[:, c * CH:(c + 1) * CH]),
                                wait=[("PE", peq)], inc="VE")
                            pek = None
                            for d in range(DT):
                                pek = P.op("tensor", lambda t, d=d, c=c: t.matmul(
                                    pw1[:, 0:CH], wkp[:, d * 128:(d + 1) * 128],
                                    Hs(d, c * CH, (c + 1) * CH),
                                    start=(d == 0), stop=(d == DT - 1)),
                                    wait=([widefree_s[1], ("VE", ve_evack)]
                                          if d == 0 else []),
                                    inc="PE" if d == DT - 1 else None)
                            ve_evack = P.op("vector", lambda v, c=c: v.tensor_add(
                                KP[:, c * CH:(c + 1) * CH], pw1[:, 0:CH],
                                pos2b[:, c * CH:(c + 1) * CH]),
                                wait=[("PE", pek)], inc="VE")
                        for tt in range(NT):
                            bk = (pb6, pb7)[tt % 2]
                            pev = None
                            for d in range(DT):
                                pev = P.op("tensor", lambda t, d=d, tt=tt, bk=bk: t.matmul(
                                    bk[:, 0:128], Hs(d, tt * 128, (tt + 1) * 128),
                                    wvp[:, d * 128:(d + 1) * 128],
                                    start=(d == 0), stop=(d == DT - 1)),
                                    wait=[("VE", ve_pb67[tt % 2])] if d == 0 else [],
                                    inc="PE" if d == DT - 1 else None)
                            wv1 = P.op("vector", lambda v, tt=tt, bk=bk: v.tensor_copy(
                                VP[:, tt * 130: tt * 130 + 64], bk[:, 0:64]),
                                wait=[("PE", pev)], inc="VE")
                            ve_pb67[tt % 2] = P.op("vector", lambda v, tt=tt, bk=bk: v.tensor_copy(
                                VP[:, tt * 130 + 65: tt * 130 + 129], bk[:, 64:128]),
                                wait=[("VE", wv1)], inc="VE")
                        pe_lastuse_w = P.at("PE")
                        widefree_s = [("VE", ve_evacq), ("VE", ve_evack)]

                        for j in range(C):
                            kn = 4 * j + 4
                            npair = kn // 2
                            segs = ({"hb": 0, "wide": pw0, "pvb": pb4, "bcb": pb6,
                                     "uo": 0, "rr": rrb[0:1, 0:CH],
                                     "bcs": bc_sb[0:64, 0:CH]},
                                    {"hb": 64, "wide": pw1, "pvb": pb5, "bcb": pb7,
                                     "uo": 2048, "rr": rrb[0:1, CH:2 * CH],
                                     "bcs": bc_sb[0:64, CH:2 * CH]})
                            se_exp = [{}, {}]

                            def emit_spair(pi):
                                masked = 2 * pi >= 4 * j
                                pe_close = [None, None]
                                for half in (0, 1):
                                    kt = 2 * pi + half
                                    for sg in (0, 1):
                                        S_ = segs[sg]
                                        wide = S_["wide"]
                                        hb = S_["hb"]
                                        w = []
                                        if half == 0:
                                            w.append(widefree_s[sg])
                                            if pi == 0 and j == 0 and sg == 0:
                                                w.append(("WQ", wq_pm))
                                        last = (half == 1) and not masked
                                        mm = P.op("tensor", lambda t, kt=kt, hb=hb, j=j, masked=masked, wide=wide, half=half: t.matmul(
                                            wide[:, half * CH:(half + 1) * CH],
                                            KP[hb:hb + 64, kt * 128:(kt + 1) * 128],
                                            QP[hb:hb + 64, j * CH:(j + 1) * CH],
                                            start=True, stop=not masked), wait=w,
                                            inc="PE" if last else None)
                                        if last:
                                            pe_close[sg] = mm
                                if masked:
                                    for half in (0, 1):
                                        kt = 2 * pi + half
                                        for sg in (0, 1):
                                            wide = segs[sg]["wide"]
                                            mm = P.op("tensor", lambda t, r=kt - 4 * j, wide=wide, half=half: t.matmul(
                                                wide[:, half * CH:(half + 1) * CH],
                                                ident[:, :],
                                                masksb[:, r * CH:(r + 1) * CH],
                                                start=False, stop=True),
                                                inc="PE" if half == 1 else None)
                                            if half == 1:
                                                pe_close[sg] = mm
                                for sg in (0, 1):
                                    S_ = segs[sg]
                                    wide = S_["wide"]
                                    se_exp[sg][pi] = P.op("scalar", lambda sc, wide=wide, uo=S_["uo"] + (pi % 2) * 1024: sc.activation(
                                        U[:, uo:uo + 1024], wide[:, 0:1024],
                                        AF.Exp, scale=SCALE),
                                        wait=[("PE", pe_close[sg]), ("PE", pe_uslot_s[sg][pi % 2])],
                                        inc="SE")
                                    widefree_s[sg] = ("SE", se_exp[sg][pi])

                            def emit_pvpair(sg, pi):
                                S_ = segs[sg]
                                hb = S_["hb"]
                                pvb = S_["pvb"]
                                for half in (0, 1):
                                    kt = 2 * pi + half
                                    w = [("SE", se_exp[sg][pi])] if half == 0 else []
                                    if kt == 0:
                                        w.append(("VE", ve_outall_s[sg]))
                                        w.append(("GE", ge_vones))
                                    P.op("tensor", lambda t, kt=kt, hb=hb, kn=kn, pvb=pvb, uo=S_["uo"] + (pi % 2) * 1024, half=half: t.matmul(
                                        pvb[0:65, 0:CH],
                                        VP[:, kt * 130 + 65 * (hb // 64): kt * 130 + 65 * (hb // 64) + 65],
                                        U[:, uo + half * CH: uo + (half + 1) * CH],
                                        start=(kt == 0), stop=(kt == kn - 1)),
                                        wait=w,
                                        inc="PE" if half == 1 else None)
                                pe_uslot_s[sg][pi % 2] = P.at("PE")

                            emit_spair(0)
                            for pi in range(1, npair):
                                emit_spair(pi)
                                emit_pvpair(0, pi - 1)
                                emit_pvpair(1, pi - 1)
                            emit_pvpair(0, npair - 1)
                            emit_pvpair(1, npair - 1)
                            for sg in (0, 1):
                                S_ = segs[sg]
                                def _rr(v, S_=S_):
                                    with nc.allow_low_precision(
                                            reason="bf16 softmax normalizer"):
                                        return v.reciprocal(S_["rr"],
                                                            S_["pvb"][64:65, 0:CH])
                                ve_rr = P.op("vector", _rr,
                                             wait=[("PE", max(pe_uslot_s[sg])),
                                                   ("PE", pe_bcr_s[sg])], inc="VE")
                                pe_bcr_s[sg] = P.op("tensor", lambda t, S_=S_: t.matmul(
                                    S_["bcb"][0:64, 0:CH], onesrow_b[0:1, 0:64],
                                    S_["rr"], start=True, stop=True),
                                    wait=[("VE", ve_rr), ("VE", ve_pb67[sg])],
                                    inc="PE")
                                wbc = P.op("vector", lambda v, S_=S_: v.tensor_copy(
                                    S_["bcs"], S_["bcb"][0:64, 0:CH]),
                                    wait=[("PE", pe_bcr_s[sg])], inc="VE")
                                ve_pb67[sg] = wbc
                                ve_outall_s[sg] = P.op("vector", lambda v, p=p, S_=S_, j=j: v.tensor_mul(
                                    out_all[S_["hb"]:S_["hb"] + 64,
                                            p * S + j * CH: p * S + (j + 1) * CH],
                                    S_["pvb"][0:64, 0:CH], S_["bcs"]),
                                    wait=[("VE", wbc)], inc="VE")
                    if oadbg is not None:
                        P.op("sync", lambda sy: sy.dma_start(
                            oadbg[:, :], out_all[:, :]),
                            wait=[("VE", P.at("VE"))], inc="HQ", amt=16)
                    P.flush("phaseB")
                endB = P.snap()

                with ExitStack() as pc:
                    sbC = lambda n, sh, dt: pc.enter_context(nc.sbuf_tensor(n, sh, dt))
                    xchunk = sbC("xchunkC", [128, DT * CH], F32)
                    sq = sbC("sqC", [128, 2 * CH], F32)
                    tmpf = sbC("tmpfC", [128, CH], F32)
                    wosb = sbC("wosb", [128, NP * D], BF16)
                    b2sb = sbC("b2sb", [8, D], BF16)
                    lg_sb = sbC("lg_sb", [128, NE], F32)
                    h2fp = sbC("h2fp", [128, DT * 128], F32)
                    wrf = sbC("wrf", [128, DT * NE], F32)
                    brf = sbC("brf", [1, NE], F32)
                    e8_sb = sbC("e8_sb", [128, NE], F32)
                    mgf = sbC("mgf", [128, NE], F32)
                    statc = sbC("statc", [128, 8], F32)
                    ttileC = sbC("ttileC", [128, 2 * D], BF16)

                    P.fence(endB, engines=("gpsimd",))
                    P.op("gpsimd", lambda g: g.dma_start(
                        wosb.rearrange("pp (pr m) -> pp pr m", m=D),
                        wo.rearrange("(pr pp) m -> pp pr m", pp=128)),
                        inc="WQ", amt=16)
                    wq_c = P.op("gpsimd", lambda g: g.dma_start(b2sb[:, :], b2[:, :]),
                                inc="WQ", amt=16)
                    P.fence(endB, engines=("tensor", "vector", "scalar", "sync"))
                    P.op("sync", lambda sy: sy.dma_start(
                        wrf.rearrange("pp (d m) -> pp d m", m=NE),
                        wr.rearrange("(d pp) m -> pp d m", pp=128)),
                        inc="LQ", amt=16)
                    lq_wrf = P.op("sync", lambda sy: sy.dma_start(brf[:, :], brr[:, :]),
                                  inc="LQ", amt=16)

                    st["ve_rows"] = endB["VE"]
                    st["pe_sq"] = [endB["PE"], endB["PE"]]
                    ve_done = 0
                    oq_prev = 0
                    ve_lgevac = 0
                    ve_ptbf = 0
                    ve_x1 = {0: endB["VE"], 1: endB["VE"]}
                    hq_relay = {0: 0, 1: 0}
                    tq_rel = {0: 0, 1: 0}
                    psO = lambda k: pw0[:, (k % 2) * 512:(k % 2) * 512 + CH]
                    for c in range(C):
                        lq = P.op("sync", lambda sy, c=c: sy.dma_start(
                            xchunk.rearrange("p (d c) -> p d c", c=CH),
                            r3(xT)[:, :, c * CH:(c + 1) * CH]),
                            wait=[("VE", ve_done), ("OQ", oq_prev)], inc="LQ", amt=16)
                        for d in range(DT):
                            peo = None
                            for pr in range(NP):
                                peo = P.op("tensor", lambda t, d=d, pr=pr, c=c: t.matmul(
                                    psO(d),
                                    wosb[:, pr * D + d * 128: pr * D + (d + 1) * 128],
                                    out_all[:, pr * S + c * CH: pr * S + (c + 1) * CH],
                                    start=(pr == 0), stop=(pr == NP - 1)),
                                    wait=([("WQ", wq_c), ("VE", ve_x1[d % 2])]
                                          if pr == 0 else []),
                                    inc="PE" if pr == NP - 1 else None)
                            ve_x1[d % 2] = P.op("vector", lambda v, d=d: v.tensor_add(
                                xchunk[:, d * CH:(d + 1) * CH], psO(d),
                                xchunk[:, d * CH:(d + 1) * CH]),
                                wait=[("PE", peo), ("LQ", lq)], inc="VE")
                        if x1dbg is not None:
                            P.op("sync", lambda sy, c=c: sy.dma_start(
                                r3(x1dbg)[:, :, c * CH:(c + 1) * CH],
                                xchunk.rearrange("p (d c) -> p d c", c=CH)),
                                wait=[("VE", ve_x1[0]), ("VE", ve_x1[1])],
                                inc="HQ", amt=16)
                        ln_chunk(c, xchunk, Hbig, sq, tmpf, rowbuf,
                                 [("LQ", lq), ("VE", max(ve_x1[0], ve_x1[1]))])
                        ve_h2 = P.at("VE")
                        if x1dbg is not None:
                            P.op("sync", lambda sy, c=c: sy.dma_start(
                                r3(h2d)[:, :, c * CH:(c + 1) * CH],
                                Hbig.rearrange("p (d s) -> p d s", s=S)[:, :, c * CH:(c + 1) * CH]),
                                wait=[("VE", ve_h2)], inc="HQ", amt=16)
                        for tt4 in range(4):
                            tt = 4 * c + tt4
                            half = tt % 2
                            for dblk in range(DT):
                                tq_rel[half] = P.op("sync", lambda en, tt=tt, half=half, dblk=dblk: en.dma_start_transpose(
                                    ttileC[:, half * D + dblk * 128: half * D + (dblk + 1) * 128],
                                    Hbig.rearrange("p (d s) -> p d s", s=S)[:, dblk, 128 * tt:128 * (tt + 1)]),
                                    wait=(([("VE", ve_h2)] if dblk == 0 else []) +
                                          ([("HQ", hq_relay[half])] if dblk == 0 else [])),
                                    inc="TQ%d" % half, amt=16)
                            hq_relay[half] = P.op("sync", lambda sy, tt=tt, half=half: sy.dma_start(
                                h2tm[1 + 128 * tt:1 + 128 * (tt + 1), :],
                                ttileC[:, half * D:(half + 1) * D]),
                                wait=[("TQ%d" % half, tq_rel[half])], inc="HQ", amt=16)
                        for tt in range(4):
                            ti = 4 * c + tt
                            ve_h2fp = None
                            for d in range(DT):
                                wm2 = P.op("vector", lambda v, d=d, tt=tt: v.tensor_mul(
                                    tmpf[:, 0:128], pb4[:, tt * 128:(tt + 1) * 128],
                                    xchunk[:, d * CH + tt * 128: d * CH + (tt + 1) * 128]),
                                    wait=([("VE", ve_lgevac)] if d == 0 else [])
                                    + ([("VE", ve_h2fp)] if d else []),
                                    inc="VE")
                                ve_h2fp = P.op("vector", lambda v, d=d, tt=tt: v.tensor_add(
                                    h2fp[:, d * 128:(d + 1) * 128],
                                    pb5[:, tt * 128:(tt + 1) * 128], tmpf[:, 0:128]),
                                    wait=[("VE", wm2)], inc="VE")
                            for d in range(DT):
                                P.op("tensor", lambda t, d=d: t.matmul(
                                    pb6[:, 0:NE], h2fp[:, d * 128:(d + 1) * 128],
                                    wrf[:, d * NE:(d + 1) * NE],
                                    start=(d == 0), stop=False),
                                    wait=([("VE", ve_h2fp), ("LQ", lq_wrf)]
                                          if d == 0 else []))
                            pe_lg = P.op("tensor", lambda t: t.matmul(
                                pb6[:, 0:NE], onesrow_f[0:1, :], brf[0:1, :],
                                start=False, stop=True), inc="PE")
                            wlg = P.op("vector", lambda v: v.tensor_copy(lg_sb[:, :], pb6[:, 0:NE]),
                                       wait=[("PE", pe_lg)], inc="VE")
                            wmx = P.op("vector", lambda v: v.reduce_max(
                                statc[:, 0:1], lg_sb[:, :], axis=AX.X),
                                wait=[("VE", wlg)], inc="VE")
                            ve_lgevac = P.op("vector", lambda v: v.tensor_scalar_mul(
                                statc[:, 1:2], statc[:, 0:1], -1.0),
                                wait=[("VE", wmx)], inc="VE")
                            se_e8 = P.op("scalar", lambda sc: sc.activation(
                                e8_sb[:, :], lg_sb[:, :], AF.Exp, bias=statc[:, 1:2]),
                                wait=[("VE", ve_lgevac)], inc="SE")
                            wsm = P.op("vector", lambda v: v.reduce_sum(
                                statc[:, 2:3], e8_sb[:, :], axis=AX.X),
                                wait=[("SE", se_e8)], inc="VE")
                            ve_g = P.op("vector", lambda v: v.reciprocal(
                                statc[:, 3:4], statc[:, 2:3]),
                                wait=[("VE", wsm)], inc="VE")
                            ve_mg = P.op("vector", lambda v: v.tensor_scalar(
                                mgf[:, :], lg_sb[:, :], statc[:, 0:1], statc[:, 3:4],
                                ALU.is_equal, ALU.mult),
                                wait=[("VE", ve_g)], inc="VE")
                            pe_tp = P.op("tensor", lambda t: t.transpose(
                                pb7[0:8, 0:128], mgf[:, :], identf[:, :]),
                                wait=[("VE", ve_mg), ("VE", ve_ptbf), ("LQ", wq_ident)],
                                inc="PE")
                            ve_ptbf = P.op("vector", lambda v, ti=ti: v.tensor_copy(
                                maskgT[0:8, ti * 128:(ti + 1) * 128], pb7[0:8, 0:128]),
                                wait=[("PE", pe_tp)], inc="VE")
                        ve_maskgT_c = ve_ptbf
                        ve_ob = dict(ve_x1)
                        for d in range(DT):
                            pe_b2 = P.op("tensor", lambda t, d=d, c=c: t.matmul(
                                psO(d), b2sb[0:8, d * 128:(d + 1) * 128],
                                maskgT[0:8, c * CH:(c + 1) * CH],
                                start=True, stop=True),
                                wait=[("VE", ve_maskgT_c), ("VE", ve_ob[d % 2])],
                                inc="PE")
                            ve_ob[d % 2] = P.op("vector", lambda v, d=d: v.tensor_add(
                                xchunk[:, d * CH:(d + 1) * CH], psO(d),
                                xchunk[:, d * CH:(d + 1) * CH]),
                                wait=[("PE", pe_b2)], inc="VE")
                        ve_done = P.at("VE")
                        ve_x1 = dict(ve_ob)
                        if basedbg is not None:
                            P.op("sync", lambda sy, c=c: sy.dma_start(
                                r3(basedbg)[:, :, c * CH:(c + 1) * CH],
                                xchunk.rearrange("p (d c) -> p d c", c=CH)),
                                wait=[("VE", ve_done)], inc="HQ", amt=16)
                        oq_prev = P.op("sync", lambda sy, c=c: sy.dma_start(
                            r3(outT)[:, :, c * CH:(c + 1) * CH],
                            xchunk.rearrange("p (d c) -> p d c", c=CH)),
                            wait=[("VE", ve_done)], inc="OQ", amt=16)
                    if h2dbg is not None:
                        P.op("sync", lambda sy: sy.dma_start(
                            h2dbg[:, :], maskgT[:, :]),
                            wait=[("VE", P.at("VE"))], inc="HQ", amt=16)
                    P.flush("phaseC")
                endC = P.snap()

        SPAD = S + 1024
        NTILE_MAX = (S + 1024) // 128
        w1sb = ctx.enter_context(nc.sbuf_tensor("w1sb", [128, DT * DFF], BF16))
        w2sb = ctx.enter_context(nc.sbuf_tensor("w2sb", [128, NF * D], BF16))
        with ExitStack() as prr:
            sbR = lambda n, sh, dt: prr.enter_context(nc.sbuf_tensor(n, sh, dt))
            B01 = sbR("B01", [8, S], F32)
            CUM = sbR("CUM", [8, S], F32)
            Z8 = sbR("Z8", [8, S], F32)
            rows8 = sbR("rows8", [8, 16], F32)
            irow8 = sbR("irow8", [8, 4], I32)
            icol8 = sbR("icol8", [8, 2], I32)
            grstg = sbR("grstg", [1, CH], F32)
            pfrow = sbR("pfrow", [1, 512], I32)
            idgrid = sbR("idgrid", [128, NT], I32)
            dstcA = sbR("dstcA", [128, NT], I32)
            ltri_sb = sbR("ltri_sb", [8, 8], F32)

            P.fence(endC, engines=P.ENGINES)
            lq_lt = P.op("sync", lambda sy: sy.dma_start(ltri_sb[:, :], ltri[:, :]),
                         inc="LQ", amt=16)
            oq_base = P.at("OQ")
            pp_all = P.at("WQ")
            for ei in range(8):
                P.op("sync", lambda sy, ei=ei: sy.dma_start(
                    w1sb.rearrange("p (d f) -> p d f", f=DFF)[:, :, ei * 512:(ei + 1) * 512],
                    w1bf[0].rearrange("(d pp) f -> pp d f", pp=128)[:, :, ei * 512:(ei + 1) * 512]),
                    wait=[("WQ", pp_all)] if ei == 0 else [], inc="WS", amt=16)
                P.op("sync", lambda sy, ei=ei: sy.dma_start(
                    w2sb.rearrange("p (f m) -> p f m", m=D)[:, ei * 4:(ei + 1) * 4, :],
                    w2bf[0].rearrange("(f pp) m -> pp f m", pp=128)[:, ei * 4:(ei + 1) * 4, :]),
                    inc="WS", amt=16)

            ve_r = endC["VE"]
            hq_g = 0
            for ch in range(C):
                pe_g = P.op("tensor", lambda t, ch=ch: t.matmul(
                    pb6[0:1, 0:CH], onescol_b[0:8, 0:1],
                    maskgT[0:8, ch * CH:(ch + 1) * CH], start=True, stop=True),
                    wait=[("VE", ve_r)], inc="PE")
                ve_r = P.op("vector", lambda v: v.tensor_copy(
                    grstg[0:1, 0:CH], pb6[0:1, 0:CH]),
                    wait=[("PE", pe_g), ("HQ", hq_g)], inc="VE")
                hq_g = P.op("sync", lambda sy, ch=ch: sy.dma_start(
                    gateD[0:1, 1 + ch * CH:1 + (ch + 1) * CH], grstg[0:1, 0:CH]),
                    wait=[("VE", ve_r)], inc="HQ", amt=16)
            w = P.op("vector", lambda v: v.tensor_single_scalar(
                B01[:, :], maskgT[0:8, :], 0.0, ALU.is_gt),
                wait=[("VE", ve_r)], inc="VE")
            w = P.op("vector", lambda v: v.memset(Z8[:, :], 0.0),
                     wait=[("VE", w)], inc="VE")
            w = P.op("vector", lambda v: v.tensor_tensor_scan(
                CUM[:, :], B01[:, :], Z8[:, :], 0.0, ALU.add, ALU.add),
                wait=[("VE", w)], inc="VE")
            w = P.op("vector", lambda v: v.tensor_copy(rows8[:, 0:1], CUM[:, S - 1:S]),
                     wait=[("VE", w)], inc="VE")
            w = P.op("vector", lambda v: v.tensor_copy(irow8[:, 0:1], rows8[:, 0:1]),
                     wait=[("VE", w)], inc="VE")
            w = P.op("vector", lambda v: v.tensor_single_scalar(
                irow8[:, 1:2], irow8[:, 0:1], 127, ALU.add),
                wait=[("VE", w)], inc="VE")
            w = P.op("vector", lambda v: v.tensor_single_scalar(
                irow8[:, 2:3], irow8[:, 1:2], -128, ALU.bitwise_and),
                wait=[("VE", w)], inc="VE")
            w = P.op("vector", lambda v: v.tensor_single_scalar(
                irow8[:, 3:4], irow8[:, 2:3], 7, ALU.logical_shift_right),
                wait=[("VE", w)], inc="VE")
            w = P.op("vector", lambda v: v.tensor_copy(rows8[:, 4:5], irow8[:, 2:3]),
                     wait=[("VE", w)], inc="VE")
            pe_off = P.op("tensor", lambda t: t.matmul(
                pb7[0:8, 0:1], ltri_sb[:, :], rows8[:, 4:5],
                start=True, stop=True),
                wait=[("VE", w), ("LQ", lq_lt)], inc="PE")
            w = P.op("vector", lambda v: v.tensor_copy(rows8[:, 6:7], pb7[0:8, 0:1]),
                     wait=[("PE", pe_off)], inc="VE")
            w = P.op("vector", lambda v: v.tensor_scalar(
                CUM[:, :], CUM[:, :], rows8[:, 6:7], -1.0, ALU.add, ALU.add),
                wait=[("VE", w)], inc="VE")
            ve_dsel = P.op("vector", lambda v: v.tensor_mul(B01[:, :], CUM[:, :], B01[:, :]),
                           wait=[("VE", w)], inc="VE")
            pe_dst = None
            for tt in range(NT):
                pe_dst = P.op("tensor", lambda t, tt=tt: t.matmul(
                    pb7[:, tt:tt + 1], B01[0:8, 128 * tt:128 * (tt + 1)],
                    onescol_f[0:8, 0:1], start=True, stop=True),
                    wait=[("VE", ve_dsel)] if tt == 0 else [],
                    inc="PE" if tt == NT - 1 else None)
            ve_dst = P.op("vector", lambda v: v.tensor_copy(
                dstcA[:, 0:NT], pb7[:, 0:NT]),
                wait=[("PE", pe_dst)], inc="VE")
            w = P.op("vector", lambda v: v.tensor_copy(icol8[:, 0:1], irow8[:, 3:4]),
                     wait=[("VE", ve_dsel)], inc="VE")
            w = P.op("vector", lambda v: v.tensor_copy(icol8[:, 1:2], rows8[:, 6:7]),
                     wait=[("VE", w)], inc="VE")
            hq_int = P.op("sync", lambda sy: sy.dma_start(intd[:, :], icol8[:, :]),
                          wait=[("VE", w)], inc="HQ", amt=16)
            lq_int = P.op("sync", lambda sy: sy.dma_start(
                introw[0:1, 0:16], bass.AP(intd, 0, [[16, 1], [1, 16]])),
                wait=[("HQ", hq_int)], inc="LQ", amt=16)
            ge_pf = P.op("gpsimd", lambda g: g.memset(pfrow[:, :], 1 << 27), inc="GE")
            hq_pf = 0
            for k in range(SPAD // 512):
                hq_pf = P.op("sync", lambda sy, k=k: sy.dma_start(
                    srcd[0:1, k * 512:(k + 1) * 512], pfrow[:, :]),
                    wait=[("GE", ge_pf)] if k == 0 else [], inc="HQ", amt=16)
            wio = P.op("gpsimd", lambda g: g.iota(
                idgrid[:, :], [[128, NT]], base=1, channel_multiplier=1),
                inc="GE")
            for tt in range(NT):
                P.op("gpsimd", lambda g, tt=tt: g.indirect_dma_start(
                    srcd.rearrange("a (g o) -> (a g) o", o=1),
                    IndirectOffsetOnAxis(dstcA[:, tt:tt + 1], 0),
                    idgrid[:, tt:tt + 1], None),
                    wait=[("HQ", hq_pf), ("GE", wio), ("VE", ve_dst)] if tt == 0
                    else [],
                    inc="AQ", amt=16)
            P.flush("phaseR")
        endR = P.snap()

        with ExitStack() as pdd:
            sbD = lambda n, sh, dt: pdd.enter_context(nc.sbuf_tensor(n, sh, dt))
            h1sb = sbD("h1sb", [128, NF * 128], BF16)
            gtm = sbD("gtm", [128, 2 * D], BF16)
            gfm = sbD("gfm", [128, 2 * D], BF16)
            ytile = sbD("ytile", [128, 2 * D], BF16)
            srcc = sbD("srcc", [128, 2], I32)
            gatec = sbD("gatec", [128, 2], F32)
            b1sb = sbD("b1sb", [128, NF * NE], F32)

            P.fence(endR, engines=P.ENGINES)
            lq_b1R = P.op("sync", lambda sy: sy.dma_start(b1sb[:, :], b1l[:, :]),
                          inc="LQ", amt=16)
            lq_dyn = max(lq_int, lq_b1R)
            ge_sc = P.op("gpsimd", lambda g: g.memset(srcc[:, :], 1 << 27))
            P.op("gpsimd", lambda g: g.memset(gtm[:, :], 0.0))
            P.op("gpsimd", lambda g: g.memset(gatec[:, :], 0.0))
            ge_sc = P.op("gpsimd", lambda g: g.memset(ytile[:, :], 0.0), inc="GE")
            P.fence({"GE": P.at("GE")}, engines=("sync", "tensor", "vector", "scalar"))
            P.flush("phaseDpre")
            endR2 = P.snap()


            def fn_sync(En):
                with ExitStack() as rc:
                    git = rc.enter_context(En.register("s_git"))
                    t1 = rc.enter_context(En.register("s_t1"))
                    pr_ = rc.enter_context(En.register("s_pr"))
                    lagP = rc.enter_context(En.register("s_lagP"))
                    nr = rc.enter_context(En.register("s_nr"))
                    En.reg_mov(git, 0)
                    En.reg_mov(lagP, 0)
                    for e in range(NE):
                        if e > 0:
                            En.wait_ge(P.h["EBPE"], e)
                            for ei in range(8):
                                En.dma_start(
                                    w1sb.rearrange("p (d f) -> p d f", f=DFF)[:, :, ei * 512:(ei + 1) * 512],
                                    w1bf[e].rearrange("(d pp) f -> pp d f", pp=128)[:, :, ei * 512:(ei + 1) * 512],
                                ).then_inc(P.h["WS"], 16)
                                En.dma_start(
                                    w2sb.rearrange("p (f m) -> p f m", m=D)[:, ei * 4:(ei + 1) * 4, :],
                                    w2bf[e].rearrange("(f pp) m -> pp f m", pp=128)[:, ei * 4:(ei + 1) * 4, :],
                                ).then_inc(P.h["WS"], 16)
                        En.wait_ge(P.h["LQ"], lq_dyn)
                        En.reg_load(nr, introw[0:1, 2 * e:2 * e + 1])
                        with En.Fori(0, nr) as it:
                            En.reg_div(t1, git, 2)
                            En.reg_add(t1, t1, 1)
                            En.reg_mul(t1, t1, 16)
                            En.wait_ge(P.h["DPE"], lagP)
                            En.reg_mod(pr_, git, 2)
                            with En.If_eq(pr_, 0):
                                En.wait_ge(P.h["GG0"], t1)
                                for dblk in range(DT):
                                    En.dma_start_transpose(
                                        gfm[:, dblk * 128:(dblk + 1) * 128],
                                        gtm[:, dblk * 128:(dblk + 1) * 128],
                                    ).then_inc(P.h["TQ"], 16)
                            with En.Else():
                                En.wait_ge(P.h["GG1"], t1)
                                for dblk in range(DT):
                                    En.dma_start_transpose(
                                        gfm[:, D + dblk * 128:D + (dblk + 1) * 128],
                                        gtm[:, D + dblk * 128:D + (dblk + 1) * 128],
                                    ).then_inc(P.h["TQ"], 16)
                            En.reg_mul(lagP, git, 33)
                            En.reg_add(git, git, 1)
                        En.sem_inc(P.h["EBSY"], 1)

            def fn_gpsimd(En):
                with ExitStack() as rc:
                    git = rc.enter_context(En.register("g_git"))
                    t1 = rc.enter_context(En.register("g_t1"))
                    pr_ = rc.enter_context(En.register("g_pr"))
                    nr = rc.enter_context(En.register("g_nr"))
                    offc = rc.enter_context(En.register("g_off"))
                    bc_r = rc.enter_context(En.register("g_bc"))
                    En.reg_mov(bc_r, S)
                    En.reg_mov(git, 0)
                    En.wait_ge(P.h["HQ"], endR["HQ"])
                    En.wait_ge(P.h["AQ"], endR["AQ"])
                    En.wait_ge(P.h["LQ"], lq_dyn)

                    def tilebody(par):
                        En.reg_add(t1, git, 1)
                        En.reg_div(t1, t1, 2)
                        En.reg_mul(t1, t1, 16)
                        En.wait_ge(P.h["GY%d" % par], t1)
                        En.dma_start(
                            srcc[:, par:par + 1],
                            bass.AP(srcd, offc, [[1, 128], [1, 1]]),
                        ).then_inc(P.h["GS"], 16)
                        En.reg_mul(t1, git, 16)
                        En.reg_add(t1, t1, 16)
                        En.wait_ge(P.h["GS"], t1)
                        En.reg_mul(t1, git, 128)
                        En.wait_ge(P.h["TQ"], t1)
                        En.indirect_dma_start(
                            gtm[:, par * D:(par + 1) * D], None,
                            h2tm[:, :],
                            IndirectOffsetOnAxis(srcc[:, par:par + 1], 0),
                            bounds_check=bc_r, oob_is_err=False,
                        ).then_inc(P.h["GG%d" % par], 16)
                        En.indirect_dma_start(
                            gatec[:, par:par + 1], None,
                            gateD.rearrange("a (g o) -> (a g) o", o=1),
                            IndirectOffsetOnAxis(srcc[:, par:par + 1], 0),
                            bounds_check=bc_r, oob_is_err=False,
                        ).then_inc(P.h["GC%d" % par], 16)
                        En.wait_ge(P.h["DVE"], git)
                        En.indirect_dma_start(
                            ytm[:, :],
                            IndirectOffsetOnAxis(srcc[:, 1 - par:2 - par], 0),
                            ytile[:, (1 - par) * D:(2 - par) * D], None,
                            bounds_check=bc_r, oob_is_err=False,
                        ).then_inc(P.h["GY%d" % (1 - par)], 16)

                    En.reg_load(nr, introw[0:1, 0:1])
                    for e in range(1, NE):
                        En.reg_load(t1, introw[0:1, 2 * e:2 * e + 1])
                        En.reg_add(nr, nr, t1)
                    En.reg_mov(offc, 0)
                    with En.Fori(0, nr) as it:
                        En.reg_mod(pr_, git, 2)
                        with En.If_eq(pr_, 0):
                            tilebody(0)
                        with En.Else():
                            tilebody(1)
                        En.reg_add(git, git, 1)
                        En.reg_add(offc, offc, 128)
                    En.sem_inc(P.h["EBGQ"], NE)
                    En.wait_ge(P.h["DVE"], git)
                    En.reg_mod(pr_, git, 2)
                    with En.If_eq(pr_, 0):
                        En.indirect_dma_start(
                            ytm[:, :], IndirectOffsetOnAxis(srcc[:, 1:2], 0),
                            ytile[:, D:2 * D], None,
                            bounds_check=bc_r, oob_is_err=False,
                        ).then_inc(P.h["GY1"], 16)
                    with En.Else():
                        En.indirect_dma_start(
                            ytm[:, :], IndirectOffsetOnAxis(srcc[:, 0:1], 0),
                            ytile[:, 0:D], None,
                            bounds_check=bc_r, oob_is_err=False,
                        ).then_inc(P.h["GY0"], 16)
                    En.reg_add(offc, git, 1)
                    En.reg_div(offc, offc, 2)
                    En.reg_mul(t1, offc, 16)
                    En.wait_ge(P.h["GY0"], t1)
                    En.reg_add(offc, offc, 1)
                    En.reg_sub(offc, offc, pr_)
                    En.reg_mul(t1, offc, 16)
                    En.wait_ge(P.h["GY1"], t1)
                    En.sem_inc(P.h["SDONE"], 1)

            def fn_pe(En):
                with ExitStack() as rc:
                    git = rc.enter_context(En.register("p_git"))
                    t1 = rc.enter_context(En.register("p_t1"))
                    t2 = rc.enter_context(En.register("p_t2"))
                    pr_ = rc.enter_context(En.register("p_pr"))
                    nr = rc.enter_context(En.register("p_nr"))
                    En.reg_mov(git, 0)
                    En.wait_ge(P.h["LQ"], lq_dyn)

                    def h1mm(f, par):
                        if f >= 2:
                            En.reg_add(t1, t2, f - 1)
                            En.wait_ge(P.h["DSE"], t1)
                        for d in range(DT):
                            mm = En.matmul(
                                pw0[:, (f % 2) * 512:(f % 2) * 512 + 128],
                                w1sb[:, d * DFF + f * 128:d * DFF + (f + 1) * 128],
                                gfm[:, par * D + d * 128:par * D + (d + 1) * 128],
                                start=(d == 0), stop=(d == DT - 1),
                            )
                            if d == DT - 1:
                                mm.then_inc(P.h["DPE"], 1)

                    def yemm(f):
                        En.reg_add(t1, t2, f + 1)
                        En.wait_ge(P.h["DSE"], t1)
                        for h in range(2):
                            mm = En.matmul(
                                (pb4, pb5)[h][:, 0:CH],
                                h1sb[:, f * 128:(f + 1) * 128],
                                w2sb[:, f * D + h * 512:f * D + (h + 1) * 512],
                                start=(f == 0), stop=(f == NF - 1),
                            )
                            if f == NF - 1 and h == 1:
                                mm.then_inc(P.h["DPE"], 1)

                    def tilecompute(par):
                        h1mm(0, par)
                        for f in range(1, NF):
                            h1mm(f, par)
                            yemm(f - 1)
                        yemm(NF - 1)

                    for e in range(NE):
                        En.wait_ge(P.h["WS"], 256 * (e + 1))
                        En.reg_load(nr, introw[0:1, 2 * e:2 * e + 1])
                        with En.Fori(0, nr) as it:
                            En.reg_mul(t1, git, 128)
                            En.reg_add(t1, t1, 128)
                            En.wait_ge(P.h["TQ"], t1)
                            En.wait_ge(P.h["DVE"], git)
                            En.reg_mul(t2, git, 32)
                            En.reg_mod(pr_, git, 2)
                            with En.If_eq(pr_, 0):
                                tilecompute(0)
                            with En.Else():
                                tilecompute(1)
                            En.reg_add(git, git, 1)
                        En.sem_inc(P.h["EBPE"], 1)

            def fn_scalar(En):
                with ExitStack() as rc:
                    git = rc.enter_context(En.register("c_git"))
                    t1 = rc.enter_context(En.register("c_t1"))
                    t2 = rc.enter_context(En.register("c_t2"))
                    nr = rc.enter_context(En.register("c_nr"))
                    En.reg_mov(git, 0)
                    En.wait_ge(P.h["LQ"], lq_dyn)
                    for e in range(NE):
                        En.reg_load(nr, introw[0:1, 2 * e:2 * e + 1])
                        with En.Fori(0, nr) as it:
                            En.reg_mul(t2, git, 33)
                            for f in range(NF):
                                En.reg_add(t1, t2, f + 1)
                                En.wait_ge(P.h["DPE"], t1)
                                En.activation(
                                    h1sb[:, f * 128:(f + 1) * 128],
                                    pw0[:, (f % 2) * 512:(f % 2) * 512 + 128],
                                    AF.Relu,
                                    bias=b1sb[:, f * NE + e:f * NE + e + 1],
                                ).then_inc(P.h["DSE"], 1)
                            En.reg_add(git, git, 1)
                        En.sem_inc(P.h["EBSE"], 1)

            def fn_vector(En):
                with ExitStack() as rc:
                    git = rc.enter_context(En.register("v_git"))
                    t1 = rc.enter_context(En.register("v_t1"))
                    pr_ = rc.enter_context(En.register("v_pr"))
                    lagS = rc.enter_context(En.register("v_lagS"))
                    nr = rc.enter_context(En.register("v_nr"))
                    En.reg_mov(git, 0)
                    En.reg_mov(lagS, 0)
                    En.wait_ge(P.h["LQ"], lq_dyn)

                    def evac(par):
                        for h in range(2):
                            op = En.tensor_scalar(
                                ytile[:, par * D + h * 512:par * D + (h + 1) * 512],
                                (pb4, pb5)[h][:, 0:CH],
                                gatec[:, par:par + 1], None, ALU.mult,
                            )
                            if h == 1:
                                op.then_inc(P.h["DVE"], 1)

                    for e in range(NE):
                        En.reg_load(nr, introw[0:1, 2 * e:2 * e + 1])
                        with En.Fori(0, nr) as it:
                            En.reg_mul(t1, git, 33)
                            En.reg_add(t1, t1, 33)
                            En.wait_ge(P.h["DPE"], t1)
                            En.reg_div(t1, git, 2)
                            En.reg_add(t1, t1, 1)
                            En.reg_mul(t1, t1, 16)
                            En.reg_add(lagS, git, 1)
                            En.reg_div(lagS, lagS, 2)
                            En.reg_mul(lagS, lagS, 16)
                            En.reg_mod(pr_, git, 2)
                            with En.If_eq(pr_, 0):
                                En.wait_ge(P.h["GC0"], t1)
                                En.wait_ge(P.h["GY0"], lagS)
                                evac(0)
                            with En.Else():
                                En.wait_ge(P.h["GC1"], t1)
                                En.wait_ge(P.h["GY1"], lagS)
                                evac(1)
                            En.reg_add(git, git, 1)
                        En.sem_inc(P.h["EBVE"], 1)

            with nc.Block("phaseDdyn") as dblk:
                dblk.gpsimd(fn_gpsimd)
                dblk.sync(fn_sync)
                dblk.tensor(fn_pe)
                dblk.scalar(fn_scalar)
                dblk.vector(fn_vector)
            P.cnt["EBPE"] += NE
            P.cnt["EBSY"] += NE
            P.cnt["EBGQ"] += NE
            P.cnt["EBSE"] += NE
            P.cnt["EBVE"] += NE
            P.cnt["SDONE"] += 1
            P.cnt["WS"] += 256 * (NE - 1)

            if debug_x1:
                dbg_ye = nc.dram_tensor("dbg_ye", [128, 2 * CH], F32,
                                        kind="ExternalOutput")
                dbg_yt = nc.dram_tensor("dbg_yt", [128, 2 * D], BF16,
                                        kind="ExternalOutput")
                dbg_gfm = nc.dram_tensor("dbg_gfm", [128, 2 * D], BF16,
                                         kind="ExternalOutput")
                dbg_h1 = nc.dram_tensor("dbg_h1", [128, NF * 128], BF16,
                                        kind="ExternalOutput")
                dbg_w1 = nc.dram_tensor("dbg_w1", [128, DT * DFF], BF16,
                                        kind="ExternalOutput")
                dbg_w2 = nc.dram_tensor("dbg_w2", [128, NF * D], BF16,
                                        kind="ExternalOutput")
                yecp = pdd.enter_context(nc.sbuf_tensor("yecp", [128, 2 * CH], F32))
                P.waitonly("vector", [("SDONE", 1)])
                P.waitonly("sync", [("SDONE", 1)])
                ve_ye = P.op("vector", lambda v: v.tensor_copy(
                    yecp[:, 0:CH], pb4[:, 0:CH]), inc="VE")
                ve_ye = P.op("vector", lambda v: v.tensor_copy(
                    yecp[:, CH:2 * CH], pb5[:, 0:CH]),
                    wait=[("VE", ve_ye)], inc="VE")
                P.op("sync", lambda sy: sy.dma_start(dbg_ye[:, :], yecp[:, :]),
                     wait=[("VE", ve_ye)], inc="HQ", amt=16)
                P.op("sync", lambda sy: sy.dma_start(dbg_yt[:, :], ytile[:, :]),
                     inc="HQ", amt=16)
                P.op("sync", lambda sy: sy.dma_start(dbg_gfm[:, :], gfm[:, :]),
                     inc="HQ", amt=16)
                P.op("sync", lambda sy: sy.dma_start(dbg_h1[:, :], h1sb[:, :]),
                     inc="HQ", amt=16)
                P.op("sync", lambda sy: sy.dma_start(dbg_w1[:, :], w1sb[:, :]),
                     inc="HQ", amt=16)
                P.op("sync", lambda sy: sy.dma_start(dbg_w2[:, :], w2sb[:, :]),
                     inc="HQ", amt=16)
                P.flush("dbgdump")

            with ExitStack() as pe_:
                sbE = lambda n, sh, dt: pe_.enter_context(nc.sbuf_tensor(n, sh, dt))
                yfm = sbE("yfmE", [128, 2 * 4096], BF16)
                P.waitonly("sync", [("SDONE", 1)])
                P.waitonly("gpsimd", [("SDONE", 1)])
                aq_e = {0: 0, 1: 0}
                tq_e = {0: 0, 1: 0}
                NG = S // 512
                for g in range(NG):
                    half = g % 2
                    for dblk2 in range(DT):
                        tq_e[half] = P.op("sync", lambda sy, g=g, half=half, dblk2=dblk2: sy.dma_start_transpose(
                            yfm[:, half * 4096 + dblk2 * 512: half * 4096 + (dblk2 + 1) * 512],
                            ytm[1 + 512 * g:1 + 512 * (g + 1), dblk2 * 128:(dblk2 + 1) * 128]),
                            wait=(([("AQ%d" % half, aq_e[half])] if dblk2 == 0 else []) +
                                  ([("HQ", P.at("HQ"))] if g < 2 and dblk2 == 0
                                   else [])),
                            inc="TQ%d" % half, amt=16)
                    aq_e[half] = P.op("gpsimd", lambda gp, g=g, half=half: gp.dma_start(
                        r3(outT)[:, :, 512 * g:512 * (g + 1)],
                        yfm[:, half * 4096:(half + 1) * 4096].rearrange("p (d c) -> p d c", c=512),
                        accum_op=ALU.add),
                        wait=[("TQ%d" % half, tq_e[half]), ("OQ", oq_base)],
                        inc="AQ%d" % half, amt=16)
                P.waitonly("sync", [("OQ", P.at("OQ")), ("AQ0", P.at("AQ0")),
                                    ("AQ1", P.at("AQ1")), ("AQ", P.at("AQ")),
                                    ("LQ", P.at("LQ")), ("HQ", P.at("HQ"))])
                P.flush("phaseE")

    return nc


def _consts(S):
    t = np.arange(S, dtype=np.float32)
    i2 = np.arange(0, DH, 2, dtype=np.float32)
    scale = (1.0 / (10000.0 ** (i2 / DH) + np.float32(np.finfo(np.float32).eps))).astype(np.float32)
    ang = (t[:, None] * scale[None, :]).astype(np.float32)
    pos = np.stack([np.sin(ang), np.cos(ang)], axis=-1).reshape(S, DH).astype(np.float32)
    pos2 = np.concatenate([np.ascontiguousarray(pos.T)] * 2, axis=0)
    kk = np.arange(128)[:, None]
    qq = np.arange(CH)[None, :]
    masks = np.concatenate(
        [np.where(128 * r + kk <= qq, 0.0, NEG).astype(np.float32) for r in range(4)],
        axis=1)
    ident = np.eye(128, dtype=np.float32)
    return np.ascontiguousarray(pos2), np.ascontiguousarray(masks), ident


def kernel(L0_attn_x, L0_attn_Wq, L0_attn_Wk, L0_attn_Wv, L0_attn_Wo,
           L0_ffn_Wr, L0_ffn_br, L0_ffn_W1, L0_ffn_b1, L0_ffn_W2, L0_ffn_b2):
    x = np.asarray(L0_attn_x, dtype=np.float32)
    B, S, _ = x.shape
    pos2, masks, ident = _consts(S)
    b1 = np.asarray(L0_ffn_b1, dtype=np.float32)
    b1l = np.ascontiguousarray(
        b1.reshape(NE, NF, 128).transpose(2, 1, 0).reshape(128, NF * NE))
    shared = {
        "wq": np.ascontiguousarray(L0_attn_Wq, dtype=np.float32),
        "wk": np.ascontiguousarray(L0_attn_Wk, dtype=np.float32),
        "wv": np.ascontiguousarray(L0_attn_Wv, dtype=np.float32),
        "wo": np.ascontiguousarray(L0_attn_Wo, dtype=np.float32),
        "wr": np.ascontiguousarray(L0_ffn_Wr, dtype=np.float32),
        "brr": np.ascontiguousarray(
            np.asarray(L0_ffn_br, dtype=np.float32).reshape(1, NE)),
        "w1": np.ascontiguousarray(L0_ffn_W1, dtype=np.float32),
        "b1l": b1l,
        "w2": np.ascontiguousarray(L0_ffn_W2, dtype=np.float32),
        "b2": np.ascontiguousarray(L0_ffn_b2, dtype=np.float32),
        "pos2": pos2, "masksd": masks, "identd": ident,
        "ltri": np.ascontiguousarray(np.triu(np.ones((8, 8), dtype=np.float32), 1)),
    }
    nc = build(S)
    in_maps = []
    for i in range(B):
        m = dict(shared)
        m["xT"] = np.ascontiguousarray(x[i].T)
        in_maps.append(m)
    res = run_bass_kernel_spmd(nc, in_maps, core_ids=list(range(8)))
    out = np.stack([np.ascontiguousarray(np.asarray(r["outT"]).T)
                    for r in res.results], axis=0)
    return out.astype(np.float32)


# version 15
# speedup vs baseline: 1.0310x; 1.0079x over previous
import numpy as np
import concourse.bass as bass
import concourse.mybir as mybir
from concourse.bass_utils import run_bass_kernel_spmd
from concourse.bass_core_rust import IndirectOffsetOnAxis
from contextlib import ExitStack

F32 = mybir.dt.float32
I32 = mybir.dt.int32
BF16 = mybir.dt.bfloat16
AF = mybir.ActivationFunctionType
ALU = mybir.AluOpType
AX = mybir.AxisListType

D = 1024
DT = 8
NP = 8
NE = 8
DFF = 4096
NF = 32
CH = 512
DH = 64
EPS = 1e-5
NEG = -1e30
SCALE = 1024.0 ** -0.5


class Plan:

    ENGINES = ("tensor", "vector", "scalar", "gpsimd", "sync")

    def __init__(self, nc, ctx):
        self.nc = nc
        self.pend = {e: [] for e in self.ENGINES}
        self.cnt = {}
        self.h = {}
        for name in (["LQ", "HQ", "OQ", "AQ", "WQ", "PE", "VE", "SE", "GE",
                      "GQ2", "TQ", "DPE", "DSE", "DVE", "EBPE", "EBGQ", "EBSY",
                      "EBSE", "EBVE", "SDONE", "WS",
                      "GS", "GG0", "GG1", "GC0", "GC1", "GY0", "GY1",
                      "AQ0", "AQ1", "TQ0", "TQ1"]):
            self.h[name] = ctx.enter_context(nc.semaphore(name))
            self.cnt[name] = 0

    def op(self, eng, fn, wait=(), inc=None, amt=1):
        w = [(s, v) for (s, v) in wait if v > 0]
        for s, v in w:
            assert v <= self.cnt[s], f"wait on future {s}:{v}>{self.cnt[s]}"
        after = None
        if inc is not None:
            self.cnt[inc] += amt
            after = self.cnt[inc]

        def run(En, w=w, fn=fn, inc=inc, amt=amt):
            for s, v in w:
                En.wait_ge(self.h[s], v)
            r = fn(En)
            if inc is not None:
                r.then_inc(self.h[inc], amt)

        self.pend[eng].append(run)
        return after

    def waitonly(self, eng, wait):
        w = [(s, v) for (s, v) in wait if v > 0]
        if not w:
            return
        for s, v in w:
            assert v <= self.cnt[s]

        def run(En, w=w):
            for s, v in w:
                En.wait_ge(self.h[s], v)

        self.pend[eng].append(run)

    def at(self, s):
        return self.cnt[s]

    def snap(self):
        return dict(self.cnt)

    def fence(self, snap, engines=ENGINES):
        for e in engines:
            self.waitonly(e, list(snap.items()))

    def flush(self, name=None):
        with self.nc.Block(name) as b:
            for eng in self.ENGINES:
                ops = self.pend[eng]
                if ops:
                    getattr(b, eng)(lambda En, ops=ops: [f(En) for f in ops])
        self.pend = {e: [] for e in self.ENGINES}


def build(S=4096, debug_x1=False):
    assert S % 512 == 0
    C = S // CH
    NT = S // 128
    nc = bass.Bass("TRN2", target_bir_lowering=False)
    try:
        nc.disable_value_cache = True
    except Exception:
        pass
    x1dbg = (nc.dram_tensor("x1dbg", [D, S], F32, kind="ExternalOutput")
             if debug_x1 else None)
    basedbg = (nc.dram_tensor("basedbg", [D, S], F32, kind="ExternalOutput")
               if debug_x1 else None)
    oadbg = (nc.dram_tensor("oadbg", [128, NP * S], BF16, kind="ExternalOutput")
             if debug_x1 else None)
    h2dbg = (nc.dram_tensor("h2dbg", [8, S], BF16, kind="ExternalOutput")
             if debug_x1 else None)

    xT = nc.dram_tensor("xT", [D, S], F32, kind="ExternalInput")
    wq = nc.dram_tensor("wq", [D, D], F32, kind="ExternalInput")
    wk = nc.dram_tensor("wk", [D, D], F32, kind="ExternalInput")
    wv = nc.dram_tensor("wv", [D, D], F32, kind="ExternalInput")
    wo = nc.dram_tensor("wo", [D, D], F32, kind="ExternalInput")
    wr = nc.dram_tensor("wr", [D, NE], F32, kind="ExternalInput")
    brr = nc.dram_tensor("brr", [1, NE], F32, kind="ExternalInput")
    w1 = nc.dram_tensor("w1", [NE, D, DFF], F32, kind="ExternalInput")
    b1l = nc.dram_tensor("b1l", [128, NF * NE], F32, kind="ExternalInput")
    w2 = nc.dram_tensor("w2", [NE, DFF, D], F32, kind="ExternalInput")
    b2 = nc.dram_tensor("b2", [NE, D], F32, kind="ExternalInput")
    pos2 = nc.dram_tensor("pos2", [128, S], F32, kind="ExternalInput")
    masksd = nc.dram_tensor("masksd", [128, 4 * CH], F32, kind="ExternalInput")
    identd = nc.dram_tensor("identd", [128, 128], F32, kind="ExternalInput")
    h2d = nc.dram_tensor("h2d", [D, S], BF16,
                         **({"kind": "ExternalOutput"} if debug_x1 else {}))
    w1bf = nc.dram_tensor("w1bf", [NE, D, DFF], BF16)
    w2bf = nc.dram_tensor("w2bf", [NE, DFF, D], BF16)
    dbgkind = {"kind": "ExternalOutput"} if debug_x1 else {}
    h2tm = nc.dram_tensor("h2tm", [S + 1, D], BF16, **dbgkind)
    ytm = nc.dram_tensor("ytm", [S + 1, D], BF16, **dbgkind)
    srcd = nc.dram_tensor("srcd", [1, S + 1024], mybir.dt.int32, **dbgkind)
    gateD = nc.dram_tensor("gateD", [1, S + 1], F32, **dbgkind)
    intd = nc.dram_tensor("intd", [8, 2], mybir.dt.int32,
                          **({"kind": "ExternalOutput"} if debug_x1 else {}))
    ltri = nc.dram_tensor("ltri", [8, 8], F32, kind="ExternalInput")
    outT = nc.dram_tensor("outT", [D, S], F32, kind="ExternalOutput")

    r3 = lambda t: t.rearrange("(dt pp) s -> pp dt s", pp=128)

    with ExitStack() as ctx:
        P = Plan(nc, ctx)
        sb = lambda n, sh, dt: ctx.enter_context(nc.sbuf_tensor(n, sh, dt))

        onesrow_f = sb("onesrow_f", [1, 128], F32)
        onesrow_b = sb("onesrow_b", [1, 128], BF16)
        onescol_f = sb("onescol_f", [128, 1], F32)
        onescol_b = sb("onescol_b", [128, 1], BF16)
        ident = sb("ident", [128, 128], BF16)
        identf = sb("identf", [128, 128], F32)
        maskgT = sb("maskgT", [8, S], BF16)
        introw = sb("introw", [1, 32], I32)
        pw0 = ctx.enter_context(nc.psum_tensor("pw0", [128, 1024], F32))
        pw1 = ctx.enter_context(nc.psum_tensor("pw1", [128, 1024], F32))
        pb4 = ctx.enter_context(nc.psum_tensor("pb4", [128, 512], F32))
        pb5 = ctx.enter_context(nc.psum_tensor("pb5", [128, 512], F32))
        pb6 = ctx.enter_context(nc.psum_tensor("pb6", [128, 512], F32))
        pb7 = ctx.enter_context(nc.psum_tensor("pb7", [128, 512], F32))

        P.op("gpsimd", lambda g: g.memset(onesrow_f[:, :], 1.0), inc="GE")
        P.op("gpsimd", lambda g: g.memset(onesrow_b[:, :], 1.0), inc="GE")
        P.op("gpsimd", lambda g: g.memset(onescol_f[:, :], 1.0), inc="GE")
        ge_ones = P.op("gpsimd", lambda g: g.memset(onescol_b[:, :], 1.0), inc="GE")
        P.op("gpsimd", lambda g: g.dma_start(ident[:, :], identd[:, :]),
             inc="WQ", amt=16)
        wq_ident = P.op("sync", lambda sy: sy.dma_start(identf[:, :], identd[:, :]),
                        inc="LQ", amt=16)

        st = {"ve_rows": 0, "pe_sq": [0, 0]}

        def ln_chunk(c, xchunk, Hbig, sq, tmpf, rowbuf, ready):
            for d in range(DT):
                sqs = sq[:, (d % 2) * CH:(d % 2 + 1) * CH]
                xs = xchunk[:, d * CH:(d + 1) * CH]
                se_sq = P.op("scalar", lambda sc, sqs=sqs, xs=xs: sc.activation(
                    sqs, xs, AF.Square),
                    wait=(list(ready) if d == 0 else []) + [("PE", st["pe_sq"][d % 2])],
                    inc="SE")
                w0 = (list(ready) + [("GE", ge_ones), ("VE", st["ve_rows"])]
                      if d == 0 else [])
                P.op("tensor", lambda t, xs=xs, d=d: t.matmul(
                    pw1[0:1, 0:CH], onescol_f[:, :], xs,
                    start=(d == 0), stop=(d == DT - 1)), wait=w0)
                st["pe_sq"][d % 2] = P.op("tensor", lambda t, sqs=sqs, d=d: t.matmul(
                    pw1[0:1, 512:512 + CH], onescol_f[:, :], sqs,
                    start=(d == 0), stop=(d == DT - 1)),
                    wait=[("SE", se_sq)], inc="PE")
            pe_stats = P.at("PE")
            ME, MS, M2, VA, SD, RS, RM, R0 = (rowbuf[0:1, i * 512:(i + 1) * 512]
                                              for i in range(8))
            w = P.op("vector", lambda v: v.tensor_scalar_mul(ME, pw1[0:1, 0:CH], 1.0 / D),
                     wait=[("PE", pe_stats)], inc="VE")
            w = P.op("vector", lambda v: v.tensor_scalar_mul(MS, pw1[0:1, 512:512 + CH], 1.0 / D),
                     wait=[("VE", w)], inc="VE")
            w = P.op("vector", lambda v: v.tensor_mul(M2, ME, ME),
                     wait=[("VE", w)], inc="VE")
            ve_var = P.op("vector", lambda v: v.scalar_tensor_tensor(
                VA, MS, EPS, M2, ALU.add, ALU.subtract), wait=[("VE", w)], inc="VE")
            se_sd = P.op("scalar", lambda sc: sc.activation(SD, VA, AF.Sqrt),
                         wait=[("VE", ve_var)], inc="SE")
            w = P.op("vector", lambda v: v.reciprocal(R0, SD),
                     wait=[("SE", se_sd)], inc="VE")
            w = P.op("vector", lambda v: v.tensor_mul(M2, SD, R0),
                     wait=[("VE", w)], inc="VE")
            w = P.op("vector", lambda v: v.tensor_scalar(M2, M2, -1.0, 2.0,
                                                         ALU.mult, ALU.add),
                     wait=[("VE", w)], inc="VE")
            w = P.op("vector", lambda v: v.tensor_mul(RS, R0, M2),
                     wait=[("VE", w)], inc="VE")
            st["ve_rows"] = P.op("vector", lambda v: v.scalar_tensor_tensor(
                RM, ME, -1.0, RS, ALU.mult, ALU.mult), wait=[("VE", w)], inc="VE")
            P.op("tensor", lambda t: t.matmul(pb4[:, 0:CH], onesrow_f[0:1, :],
                                              RS, start=True, stop=True),
                 wait=[("VE", st["ve_rows"])])
            pe_bc = P.op("tensor", lambda t: t.matmul(pb5[:, 0:CH], onesrow_f[0:1, :],
                                                      RM, start=True, stop=True),
                         inc="PE")
            ve_h = 0
            for d in range(DT):
                xs = xchunk[:, d * CH:(d + 1) * CH]
                wm = P.op("vector", lambda v, xs=xs: v.tensor_mul(
                    tmpf[:, :], pb4[:, 0:CH], xs),
                    wait=([("PE", pe_bc)] if d == 0 else []) + ([("VE", ve_h)] if d else []),
                    inc="VE")
                ve_h = P.op("vector", lambda v, d=d: v.tensor_add(
                    Hbig[:, d * S + c * CH: d * S + (c + 1) * CH],
                    pb5[:, 0:CH], tmpf[:, :]), wait=[("VE", wm)], inc="VE")
            return ve_h

        with ExitStack() as abc:
            sbH = lambda n, sh, dt: abc.enter_context(nc.sbuf_tensor(n, sh, dt))
            Hbig = sbH("Hbig", [128, DT * S], BF16)
            rowbuf = sbH("rowbuf", [1, 4096], F32)
            Hs = lambda d, a, b: Hbig[:, d * S + a: d * S + b]

            with ExitStack() as pa:
                sbA = lambda n, sh, dt: pa.enter_context(nc.sbuf_tensor(n, sh, dt))
                xchunk = sbA("xchunkA", [128, DT * CH], F32)
                sq = sbA("sqA", [128, 2 * CH], F32)
                tmpf = sbA("tmpfA", [128, CH], F32)
                ve_h = 0
                for c in range(C):
                    lq = P.op("sync", lambda sy, c=c: sy.dma_start(
                        xchunk.rearrange("p (d c) -> p d c", c=CH),
                        r3(xT)[:, :, c * CH:(c + 1) * CH]),
                        wait=[("VE", ve_h)], inc="LQ", amt=16)
                    ve_h = ln_chunk(c, xchunk, Hbig, sq, tmpf, rowbuf, [("LQ", lq)])
                P.flush("phaseA")
            endA = P.snap()

            with ExitStack() as bcs:
                sbBC = lambda n, sh, dt: bcs.enter_context(nc.sbuf_tensor(n, sh, dt))
                out_all = sbBC("out_all", [128, NP * S], BF16)

                with ExitStack() as pbs:
                    sbB = lambda n, sh, dt: pbs.enter_context(nc.sbuf_tensor(n, sh, dt))
                    QP = sbB("QP", [128, S], BF16)
                    KP = sbB("KP", [128, S], BF16)
                    VP = sbB("VP", [128, NT * 130], BF16)
                    wqp = sbB("wqp", [128, D], BF16)
                    wkp = sbB("wkp", [128, D], BF16)
                    wvp = sbB("wvp", [128, D], BF16)
                    U = sbB("U", [128, 4 * 1024], BF16)
                    pos2b = sbB("pos2b", [128, S], BF16)
                    masksb = sbB("masksb", [128, 4 * CH], BF16)
                    bc_sb = sbB("bc_sb", [64, 2 * CH], F32)
                    rrb = rowbuf[0:1, 0:CH].bitcast(BF16)

                    P.fence(endA, engines=("gpsimd",))
                    P.op("gpsimd", lambda g: g.dma_start(pos2b[:, :], pos2[:, :]),
                         inc="WQ", amt=16)
                    wq_pm = P.op("gpsimd", lambda g: g.dma_start(masksb[:, :], masksd[:, :]),
                                 inc="WQ", amt=16)
                    ge_vones = P.op("gpsimd", lambda g: g.memset(
                        VP.rearrange("p (t w) -> p t w", w=130)[:, :, 64:65], 1.0),
                        inc="GE")
                    ge_vones = P.op("gpsimd", lambda g: g.memset(
                        VP.rearrange("p (t w) -> p t w", w=130)[:, :, 129:130], 1.0),
                        inc="GE")
                    P.fence(endA, engines=("tensor", "vector", "scalar", "sync"))

                    pe_lastuse_w = endA["PE"]
                    widefree_s = [("VE", endA["VE"]), ("VE", endA["VE"])]
                    pe_uslot_s = [[endA["PE"]] * 2, [endA["PE"]] * 2]
                    pe_bcr_s = [endA["PE"], endA["PE"]]
                    ve_outall_s = [endA["VE"], endA["VE"]]
                    ve_evacq = endA["VE"]
                    ve_evack = endA["VE"]
                    ve_pb67 = {0: endA["VE"], 1: endA["VE"]}
                    for p in range(NP):
                        wq_w = None
                        for (dst, src) in ((wqp, wq), (wkp, wk), (wvp, wv)):
                            wq_w = P.op("gpsimd", lambda g, dst=dst, src=src, p=p: g.dma_start(
                                dst.rearrange("pp (d m) -> pp d m", m=128),
                                src.rearrange("(d pp) m -> pp d m", pp=128)[:, :, p * 128:(p + 1) * 128]),
                                wait=[("PE", pe_lastuse_w)], inc="WQ", amt=16)
                        P.op("gpsimd", lambda g, p=p: g.dma_start(w1bf[p], w1[p]),
                             inc="WQ", amt=16)
                        P.op("gpsimd", lambda g, p=p: g.dma_start(w2bf[p], w2[p]),
                             inc="WQ", amt=16)
                        for c in range(C):
                            peq = None
                            for d in range(DT):
                                peq = P.op("tensor", lambda t, d=d, c=c: t.matmul(
                                    pw0[:, 0:CH], wqp[:, d * 128:(d + 1) * 128],
                                    Hs(d, c * CH, (c + 1) * CH),
                                    start=(d == 0), stop=(d == DT - 1)),
                                    wait=([("WQ", wq_w), widefree_s[0],
                                           ("VE", ve_evacq)] if d == 0 else []),
                                    inc="PE" if d == DT - 1 else None)
                            ve_evacq = P.op("vector", lambda v, c=c: v.tensor_add(
                                QP[:, c * CH:(c + 1) * CH], pw0[:, 0:CH],
                                pos2b[:, c * CH:(c + 1) * CH]),
                                wait=[("PE", peq)], inc="VE")
                            pek = None
                            for d in range(DT):
                                pek = P.op("tensor", lambda t, d=d, c=c: t.matmul(
                                    pw1[:, 0:CH], wkp[:, d * 128:(d + 1) * 128],
                                    Hs(d, c * CH, (c + 1) * CH),
                                    start=(d == 0), stop=(d == DT - 1)),
                                    wait=([widefree_s[1], ("VE", ve_evack)]
                                          if d == 0 else []),
                                    inc="PE" if d == DT - 1 else None)
                            ve_evack = P.op("vector", lambda v, c=c: v.tensor_add(
                                KP[:, c * CH:(c + 1) * CH], pw1[:, 0:CH],
                                pos2b[:, c * CH:(c + 1) * CH]),
                                wait=[("PE", pek)], inc="VE")
                        for tt in range(NT):
                            bk = (pb6, pb7)[tt % 2]
                            pev = None
                            for d in range(DT):
                                pev = P.op("tensor", lambda t, d=d, tt=tt, bk=bk: t.matmul(
                                    bk[:, 0:128], Hs(d, tt * 128, (tt + 1) * 128),
                                    wvp[:, d * 128:(d + 1) * 128],
                                    start=(d == 0), stop=(d == DT - 1)),
                                    wait=[("VE", ve_pb67[tt % 2])] if d == 0 else [],
                                    inc="PE" if d == DT - 1 else None)
                            wv1 = P.op("vector", lambda v, tt=tt, bk=bk: v.tensor_copy(
                                VP[:, tt * 130: tt * 130 + 64], bk[:, 0:64]),
                                wait=[("PE", pev)], inc="VE")
                            ve_pb67[tt % 2] = P.op("vector", lambda v, tt=tt, bk=bk: v.tensor_copy(
                                VP[:, tt * 130 + 65: tt * 130 + 129], bk[:, 64:128]),
                                wait=[("VE", wv1)], inc="VE")
                        pe_lastuse_w = P.at("PE")
                        widefree_s = [("VE", ve_evacq), ("VE", ve_evack)]

                        for j in range(C):
                            kn = 4 * j + 4
                            npair = kn // 2
                            segs = ({"hb": 0, "wide": pw0, "pvb": pb4, "bcb": pb6,
                                     "uo": 0, "rr": rrb[0:1, 0:CH],
                                     "bcs": bc_sb[0:64, 0:CH]},
                                    {"hb": 64, "wide": pw1, "pvb": pb5, "bcb": pb7,
                                     "uo": 2048, "rr": rrb[0:1, CH:2 * CH],
                                     "bcs": bc_sb[0:64, CH:2 * CH]})
                            se_exp = [{}, {}]

                            def emit_spair(pi):
                                masked = 2 * pi >= 4 * j
                                pe_close = [None, None]
                                for half in (0, 1):
                                    kt = 2 * pi + half
                                    for sg in (0, 1):
                                        S_ = segs[sg]
                                        wide = S_["wide"]
                                        hb = S_["hb"]
                                        w = []
                                        if half == 0:
                                            w.append(widefree_s[sg])
                                            if pi == 0 and j == 0 and sg == 0:
                                                w.append(("WQ", wq_pm))
                                        last = (half == 1) and not masked
                                        mm = P.op("tensor", lambda t, kt=kt, hb=hb, j=j, masked=masked, wide=wide, half=half: t.matmul(
                                            wide[:, half * CH:(half + 1) * CH],
                                            KP[hb:hb + 64, kt * 128:(kt + 1) * 128],
                                            QP[hb:hb + 64, j * CH:(j + 1) * CH],
                                            start=True, stop=not masked), wait=w,
                                            inc="PE" if last else None)
                                        if last:
                                            pe_close[sg] = mm
                                if masked:
                                    for half in (0, 1):
                                        kt = 2 * pi + half
                                        for sg in (0, 1):
                                            wide = segs[sg]["wide"]
                                            mm = P.op("tensor", lambda t, r=kt - 4 * j, wide=wide, half=half: t.matmul(
                                                wide[:, half * CH:(half + 1) * CH],
                                                ident[:, :],
                                                masksb[:, r * CH:(r + 1) * CH],
                                                start=False, stop=True),
                                                inc="PE" if half == 1 else None)
                                            if half == 1:
                                                pe_close[sg] = mm
                                for sg in (0, 1):
                                    S_ = segs[sg]
                                    wide = S_["wide"]
                                    se_exp[sg][pi] = P.op("scalar", lambda sc, wide=wide, uo=S_["uo"] + (pi % 2) * 1024: sc.activation(
                                        U[:, uo:uo + 1024], wide[:, 0:1024],
                                        AF.Exp, scale=SCALE),
                                        wait=[("PE", pe_close[sg]), ("PE", pe_uslot_s[sg][pi % 2])],
                                        inc="SE")
                                    widefree_s[sg] = ("SE", se_exp[sg][pi])

                            def emit_pvpair(sg, pi):
                                S_ = segs[sg]
                                hb = S_["hb"]
                                pvb = S_["pvb"]
                                for half in (0, 1):
                                    kt = 2 * pi + half
                                    w = [("SE", se_exp[sg][pi])] if half == 0 else []
                                    if kt == 0:
                                        w.append(("VE", ve_outall_s[sg]))
                                        w.append(("GE", ge_vones))
                                    P.op("tensor", lambda t, kt=kt, hb=hb, kn=kn, pvb=pvb, uo=S_["uo"] + (pi % 2) * 1024, half=half: t.matmul(
                                        pvb[0:65, 0:CH],
                                        VP[:, kt * 130 + 65 * (hb // 64): kt * 130 + 65 * (hb // 64) + 65],
                                        U[:, uo + half * CH: uo + (half + 1) * CH],
                                        start=(kt == 0), stop=(kt == kn - 1)),
                                        wait=w,
                                        inc="PE" if half == 1 else None)
                                pe_uslot_s[sg][pi % 2] = P.at("PE")

                            emit_spair(0)
                            for pi in range(1, npair):
                                emit_spair(pi)
                                emit_pvpair(0, pi - 1)
                                emit_pvpair(1, pi - 1)
                            emit_pvpair(0, npair - 1)
                            emit_pvpair(1, npair - 1)
                            for sg in (0, 1):
                                S_ = segs[sg]
                                def _rr(v, S_=S_):
                                    with nc.allow_low_precision(
                                            reason="bf16 softmax normalizer"):
                                        return v.reciprocal(S_["rr"],
                                                            S_["pvb"][64:65, 0:CH])
                                ve_rr = P.op("vector", _rr,
                                             wait=[("PE", max(pe_uslot_s[sg])),
                                                   ("PE", pe_bcr_s[sg])], inc="VE")
                                pe_bcr_s[sg] = P.op("tensor", lambda t, S_=S_: t.matmul(
                                    S_["bcb"][0:64, 0:CH], onesrow_b[0:1, 0:64],
                                    S_["rr"], start=True, stop=True),
                                    wait=[("VE", ve_rr), ("VE", ve_pb67[sg])],
                                    inc="PE")
                                wbc = P.op("vector", lambda v, S_=S_: v.tensor_copy(
                                    S_["bcs"], S_["bcb"][0:64, 0:CH]),
                                    wait=[("PE", pe_bcr_s[sg])], inc="VE")
                                ve_pb67[sg] = wbc
                                ve_outall_s[sg] = P.op("vector", lambda v, p=p, S_=S_, j=j: v.tensor_mul(
                                    out_all[S_["hb"]:S_["hb"] + 64,
                                            p * S + j * CH: p * S + (j + 1) * CH],
                                    S_["pvb"][0:64, 0:CH], S_["bcs"]),
                                    wait=[("VE", wbc)], inc="VE")
                    if oadbg is not None:
                        P.op("sync", lambda sy: sy.dma_start(
                            oadbg[:, :], out_all[:, :]),
                            wait=[("VE", P.at("VE"))], inc="HQ", amt=16)
                    P.flush("phaseB")
                endB = P.snap()

                with ExitStack() as pc:
                    sbC = lambda n, sh, dt: pc.enter_context(nc.sbuf_tensor(n, sh, dt))
                    xchunk = sbC("xchunkC", [128, DT * CH], F32)
                    sq = sbC("sqC", [128, 2 * CH], F32)
                    tmpf = sbC("tmpfC", [128, CH], F32)
                    wosb = sbC("wosb", [128, NP * D], BF16)
                    b2sb = sbC("b2sb", [8, D], BF16)
                    lg_sb = sbC("lg_sb", [128, NE], F32)
                    h2fp = sbC("h2fp", [128, DT * 128], F32)
                    wrf = sbC("wrf", [128, DT * NE], F32)
                    brf = sbC("brf", [1, NE], F32)
                    e8_sb = sbC("e8_sb", [128, NE], F32)
                    mgf = sbC("mgf", [128, NE], F32)
                    statc = sbC("statc", [128, 8], F32)
                    ttileC = sbC("ttileC", [128, 2 * D], BF16)

                    P.fence(endB, engines=("gpsimd",))
                    P.op("gpsimd", lambda g: g.dma_start(
                        wosb.rearrange("pp (pr m) -> pp pr m", m=D),
                        wo.rearrange("(pr pp) m -> pp pr m", pp=128)),
                        inc="WQ", amt=16)
                    wq_c = P.op("gpsimd", lambda g: g.dma_start(b2sb[:, :], b2[:, :]),
                                inc="WQ", amt=16)
                    P.fence(endB, engines=("tensor", "vector", "scalar", "sync"))
                    P.op("sync", lambda sy: sy.dma_start(
                        wrf.rearrange("pp (d m) -> pp d m", m=NE),
                        wr.rearrange("(d pp) m -> pp d m", pp=128)),
                        inc="LQ", amt=16)
                    lq_wrf = P.op("sync", lambda sy: sy.dma_start(brf[:, :], brr[:, :]),
                                  inc="LQ", amt=16)

                    st["ve_rows"] = endB["VE"]
                    st["pe_sq"] = [endB["PE"], endB["PE"]]
                    ve_done = 0
                    oq_prev = 0
                    ve_lgevac = 0
                    ve_ptbf = 0
                    ve_x1 = {0: endB["VE"], 1: endB["VE"]}
                    hq_relay = {0: 0, 1: 0}
                    tq_rel = {0: 0, 1: 0}
                    psO = lambda k: pw0[:, (k % 2) * 512:(k % 2) * 512 + CH]
                    for c in range(C):
                        lq = P.op("sync", lambda sy, c=c: sy.dma_start(
                            xchunk.rearrange("p (d c) -> p d c", c=CH),
                            r3(xT)[:, :, c * CH:(c + 1) * CH]),
                            wait=[("VE", ve_done), ("OQ", oq_prev)], inc="LQ", amt=16)
                        for d in range(DT):
                            peo = None
                            for pr in range(NP):
                                peo = P.op("tensor", lambda t, d=d, pr=pr, c=c: t.matmul(
                                    psO(d),
                                    wosb[:, pr * D + d * 128: pr * D + (d + 1) * 128],
                                    out_all[:, pr * S + c * CH: pr * S + (c + 1) * CH],
                                    start=(pr == 0), stop=(pr == NP - 1)),
                                    wait=([("WQ", wq_c), ("VE", ve_x1[d % 2])]
                                          if pr == 0 else []),
                                    inc="PE" if pr == NP - 1 else None)
                            ve_x1[d % 2] = P.op("vector", lambda v, d=d: v.tensor_add(
                                xchunk[:, d * CH:(d + 1) * CH], psO(d),
                                xchunk[:, d * CH:(d + 1) * CH]),
                                wait=[("PE", peo), ("LQ", lq)], inc="VE")
                        if x1dbg is not None:
                            P.op("sync", lambda sy, c=c: sy.dma_start(
                                r3(x1dbg)[:, :, c * CH:(c + 1) * CH],
                                xchunk.rearrange("p (d c) -> p d c", c=CH)),
                                wait=[("VE", ve_x1[0]), ("VE", ve_x1[1])],
                                inc="HQ", amt=16)
                        ln_chunk(c, xchunk, Hbig, sq, tmpf, rowbuf,
                                 [("LQ", lq), ("VE", max(ve_x1[0], ve_x1[1]))])
                        ve_h2 = P.at("VE")
                        if x1dbg is not None:
                            P.op("sync", lambda sy, c=c: sy.dma_start(
                                r3(h2d)[:, :, c * CH:(c + 1) * CH],
                                Hbig.rearrange("p (d s) -> p d s", s=S)[:, :, c * CH:(c + 1) * CH]),
                                wait=[("VE", ve_h2)], inc="HQ", amt=16)
                        for tt4 in range(4):
                            tt = 4 * c + tt4
                            half = tt % 2
                            for dblk in range(DT):
                                tq_rel[half] = P.op("sync", lambda en, tt=tt, half=half, dblk=dblk: en.dma_start_transpose(
                                    ttileC[:, half * D + dblk * 128: half * D + (dblk + 1) * 128],
                                    Hbig.rearrange("p (d s) -> p d s", s=S)[:, dblk, 128 * tt:128 * (tt + 1)]),
                                    wait=(([("VE", ve_h2)] if dblk == 0 else []) +
                                          ([("HQ", hq_relay[half])] if dblk == 0 else [])),
                                    inc="TQ%d" % half, amt=16)
                            hq_relay[half] = P.op("sync", lambda sy, tt=tt, half=half: sy.dma_start(
                                h2tm[1 + 128 * tt:1 + 128 * (tt + 1), :],
                                ttileC[:, half * D:(half + 1) * D]),
                                wait=[("TQ%d" % half, tq_rel[half])], inc="HQ", amt=16)
                        for tt in range(4):
                            ti = 4 * c + tt
                            ve_h2fp = None
                            for d in range(DT):
                                wm2 = P.op("vector", lambda v, d=d, tt=tt: v.tensor_mul(
                                    tmpf[:, 0:128], pb4[:, tt * 128:(tt + 1) * 128],
                                    xchunk[:, d * CH + tt * 128: d * CH + (tt + 1) * 128]),
                                    wait=([("VE", ve_lgevac)] if d == 0 else [])
                                    + ([("VE", ve_h2fp)] if d else []),
                                    inc="VE")
                                ve_h2fp = P.op("vector", lambda v, d=d, tt=tt: v.tensor_add(
                                    h2fp[:, d * 128:(d + 1) * 128],
                                    pb5[:, tt * 128:(tt + 1) * 128], tmpf[:, 0:128]),
                                    wait=[("VE", wm2)], inc="VE")
                            for d in range(DT):
                                P.op("tensor", lambda t, d=d: t.matmul(
                                    pb6[:, 0:NE], h2fp[:, d * 128:(d + 1) * 128],
                                    wrf[:, d * NE:(d + 1) * NE],
                                    start=(d == 0), stop=False),
                                    wait=([("VE", ve_h2fp), ("LQ", lq_wrf)]
                                          if d == 0 else []))
                            pe_lg = P.op("tensor", lambda t: t.matmul(
                                pb6[:, 0:NE], onesrow_f[0:1, :], brf[0:1, :],
                                start=False, stop=True), inc="PE")
                            wlg = P.op("vector", lambda v: v.tensor_copy(lg_sb[:, :], pb6[:, 0:NE]),
                                       wait=[("PE", pe_lg)], inc="VE")
                            wmx = P.op("vector", lambda v: v.reduce_max(
                                statc[:, 0:1], lg_sb[:, :], axis=AX.X),
                                wait=[("VE", wlg)], inc="VE")
                            ve_lgevac = P.op("vector", lambda v: v.tensor_scalar_mul(
                                statc[:, 1:2], statc[:, 0:1], -1.0),
                                wait=[("VE", wmx)], inc="VE")
                            se_e8 = P.op("scalar", lambda sc: sc.activation(
                                e8_sb[:, :], lg_sb[:, :], AF.Exp, bias=statc[:, 1:2]),
                                wait=[("VE", ve_lgevac)], inc="SE")
                            wsm = P.op("vector", lambda v: v.reduce_sum(
                                statc[:, 2:3], e8_sb[:, :], axis=AX.X),
                                wait=[("SE", se_e8)], inc="VE")
                            ve_g = P.op("vector", lambda v: v.reciprocal(
                                statc[:, 3:4], statc[:, 2:3]),
                                wait=[("VE", wsm)], inc="VE")
                            ve_mg = P.op("vector", lambda v: v.tensor_scalar(
                                mgf[:, :], lg_sb[:, :], statc[:, 0:1], statc[:, 3:4],
                                ALU.is_equal, ALU.mult),
                                wait=[("VE", ve_g)], inc="VE")
                            pe_tp = P.op("tensor", lambda t: t.transpose(
                                pb7[0:8, 0:128], mgf[:, :], identf[:, :]),
                                wait=[("VE", ve_mg), ("VE", ve_ptbf), ("LQ", wq_ident)],
                                inc="PE")
                            ve_ptbf = P.op("vector", lambda v, ti=ti: v.tensor_copy(
                                maskgT[0:8, ti * 128:(ti + 1) * 128], pb7[0:8, 0:128]),
                                wait=[("PE", pe_tp)], inc="VE")
                        ve_maskgT_c = ve_ptbf
                        ve_ob = dict(ve_x1)
                        for d in range(DT):
                            pe_b2 = P.op("tensor", lambda t, d=d, c=c: t.matmul(
                                psO(d), b2sb[0:8, d * 128:(d + 1) * 128],
                                maskgT[0:8, c * CH:(c + 1) * CH],
                                start=True, stop=True),
                                wait=[("VE", ve_maskgT_c), ("VE", ve_ob[d % 2])],
                                inc="PE")
                            ve_ob[d % 2] = P.op("vector", lambda v, d=d: v.tensor_add(
                                xchunk[:, d * CH:(d + 1) * CH], psO(d),
                                xchunk[:, d * CH:(d + 1) * CH]),
                                wait=[("PE", pe_b2)], inc="VE")
                        ve_done = P.at("VE")
                        ve_x1 = dict(ve_ob)
                        if basedbg is not None:
                            P.op("sync", lambda sy, c=c: sy.dma_start(
                                r3(basedbg)[:, :, c * CH:(c + 1) * CH],
                                xchunk.rearrange("p (d c) -> p d c", c=CH)),
                                wait=[("VE", ve_done)], inc="HQ", amt=16)
                        oq_prev = P.op("sync", lambda sy, c=c: sy.dma_start(
                            r3(outT)[:, :, c * CH:(c + 1) * CH],
                            xchunk.rearrange("p (d c) -> p d c", c=CH)),
                            wait=[("VE", ve_done)], inc="OQ", amt=16)
                    if h2dbg is not None:
                        P.op("sync", lambda sy: sy.dma_start(
                            h2dbg[:, :], maskgT[:, :]),
                            wait=[("VE", P.at("VE"))], inc="HQ", amt=16)
                    P.flush("phaseC")
                endC = P.snap()

        SPAD = S + 1024
        NTILE_MAX = (S + 1024) // 128
        w1sb = ctx.enter_context(nc.sbuf_tensor("w1sb", [128, DT * DFF], BF16))
        w2sb = ctx.enter_context(nc.sbuf_tensor("w2sb", [128, NF * D], BF16))
        with ExitStack() as prr:
            sbR = lambda n, sh, dt: prr.enter_context(nc.sbuf_tensor(n, sh, dt))
            B01 = sbR("B01", [8, S], F32)
            CUM = sbR("CUM", [8, S], F32)
            Z8 = sbR("Z8", [8, S], F32)
            rows8 = sbR("rows8", [8, 16], F32)
            irow8 = sbR("irow8", [8, 4], I32)
            icol8 = sbR("icol8", [8, 2], I32)
            grstg = sbR("grstg", [1, CH], F32)
            pfrow = sbR("pfrow", [1, 512], I32)
            idgrid = sbR("idgrid", [128, NT], I32)
            dstcA = sbR("dstcA", [128, NT], I32)
            ltri_sb = sbR("ltri_sb", [8, 8], F32)

            P.fence(endC, engines=P.ENGINES)
            lq_lt = P.op("sync", lambda sy: sy.dma_start(ltri_sb[:, :], ltri[:, :]),
                         inc="LQ", amt=16)
            oq_base = P.at("OQ")
            pp_all = P.at("WQ")
            for ei in range(8):
                P.op("sync", lambda sy, ei=ei: sy.dma_start(
                    w1sb.rearrange("p (d f) -> p d f", f=DFF)[:, :, ei * 512:(ei + 1) * 512],
                    w1bf[0].rearrange("(d pp) f -> pp d f", pp=128)[:, :, ei * 512:(ei + 1) * 512]),
                    wait=[("WQ", pp_all)] if ei == 0 else [], inc="WS", amt=16)
                P.op("sync", lambda sy, ei=ei: sy.dma_start(
                    w2sb.rearrange("p (f m) -> p f m", m=D)[:, ei * 4:(ei + 1) * 4, :],
                    w2bf[0].rearrange("(f pp) m -> pp f m", pp=128)[:, ei * 4:(ei + 1) * 4, :]),
                    inc="WS", amt=16)

            ve_r = endC["VE"]
            hq_g = 0
            for ch in range(C):
                pe_g = P.op("tensor", lambda t, ch=ch: t.matmul(
                    pb6[0:1, 0:CH], onescol_b[0:8, 0:1],
                    maskgT[0:8, ch * CH:(ch + 1) * CH], start=True, stop=True),
                    wait=[("VE", ve_r)], inc="PE")
                ve_r = P.op("vector", lambda v: v.tensor_copy(
                    grstg[0:1, 0:CH], pb6[0:1, 0:CH]),
                    wait=[("PE", pe_g), ("HQ", hq_g)], inc="VE")
                hq_g = P.op("sync", lambda sy, ch=ch: sy.dma_start(
                    gateD[0:1, 1 + ch * CH:1 + (ch + 1) * CH], grstg[0:1, 0:CH]),
                    wait=[("VE", ve_r)], inc="HQ", amt=16)
            w = P.op("vector", lambda v: v.tensor_single_scalar(
                B01[:, :], maskgT[0:8, :], 0.0, ALU.is_gt),
                wait=[("VE", ve_r)], inc="VE")
            w = P.op("vector", lambda v: v.memset(Z8[:, :], 0.0),
                     wait=[("VE", w)], inc="VE")
            w = P.op("vector", lambda v: v.tensor_tensor_scan(
                CUM[:, :], B01[:, :], Z8[:, :], 0.0, ALU.add, ALU.add),
                wait=[("VE", w)], inc="VE")
            w = P.op("vector", lambda v: v.tensor_copy(rows8[:, 0:1], CUM[:, S - 1:S]),
                     wait=[("VE", w)], inc="VE")
            w = P.op("vector", lambda v: v.tensor_copy(irow8[:, 0:1], rows8[:, 0:1]),
                     wait=[("VE", w)], inc="VE")
            w = P.op("vector", lambda v: v.tensor_single_scalar(
                irow8[:, 1:2], irow8[:, 0:1], 127, ALU.add),
                wait=[("VE", w)], inc="VE")
            w = P.op("vector", lambda v: v.tensor_single_scalar(
                irow8[:, 2:3], irow8[:, 1:2], -128, ALU.bitwise_and),
                wait=[("VE", w)], inc="VE")
            w = P.op("vector", lambda v: v.tensor_single_scalar(
                irow8[:, 3:4], irow8[:, 2:3], 7, ALU.logical_shift_right),
                wait=[("VE", w)], inc="VE")
            w = P.op("vector", lambda v: v.tensor_copy(rows8[:, 4:5], irow8[:, 2:3]),
                     wait=[("VE", w)], inc="VE")
            pe_off = P.op("tensor", lambda t: t.matmul(
                pb7[0:8, 0:1], ltri_sb[:, :], rows8[:, 4:5],
                start=True, stop=True),
                wait=[("VE", w), ("LQ", lq_lt)], inc="PE")
            w = P.op("vector", lambda v: v.tensor_copy(rows8[:, 6:7], pb7[0:8, 0:1]),
                     wait=[("PE", pe_off)], inc="VE")
            w = P.op("vector", lambda v: v.tensor_scalar(
                CUM[:, :], CUM[:, :], rows8[:, 6:7], -1.0, ALU.add, ALU.add),
                wait=[("VE", w)], inc="VE")
            ve_dsel = P.op("vector", lambda v: v.tensor_mul(B01[:, :], CUM[:, :], B01[:, :]),
                           wait=[("VE", w)], inc="VE")
            pe_dst = None
            for tt in range(NT):
                pe_dst = P.op("tensor", lambda t, tt=tt: t.matmul(
                    pb7[:, tt:tt + 1], B01[0:8, 128 * tt:128 * (tt + 1)],
                    onescol_f[0:8, 0:1], start=True, stop=True),
                    wait=[("VE", ve_dsel)] if tt == 0 else [],
                    inc="PE" if tt == NT - 1 else None)
            ve_dst = P.op("vector", lambda v: v.tensor_copy(
                dstcA[:, 0:NT], pb7[:, 0:NT]),
                wait=[("PE", pe_dst)], inc="VE")
            w = P.op("vector", lambda v: v.tensor_copy(icol8[:, 0:1], irow8[:, 3:4]),
                     wait=[("VE", ve_dsel)], inc="VE")
            w = P.op("vector", lambda v: v.tensor_copy(icol8[:, 1:2], rows8[:, 6:7]),
                     wait=[("VE", w)], inc="VE")
            hq_int = P.op("sync", lambda sy: sy.dma_start(intd[:, :], icol8[:, :]),
                          wait=[("VE", w)], inc="HQ", amt=16)
            lq_int = P.op("sync", lambda sy: sy.dma_start(
                introw[0:1, 0:16], bass.AP(intd, 0, [[16, 1], [1, 16]])),
                wait=[("HQ", hq_int)], inc="LQ", amt=16)
            ge_pf = P.op("gpsimd", lambda g: g.memset(pfrow[:, :], 1 << 27), inc="GE")
            hq_pf = 0
            for k in range(SPAD // 512):
                hq_pf = P.op("sync", lambda sy, k=k: sy.dma_start(
                    srcd[0:1, k * 512:(k + 1) * 512], pfrow[:, :]),
                    wait=[("GE", ge_pf)] if k == 0 else [], inc="HQ", amt=16)
            wio = P.op("gpsimd", lambda g: g.iota(
                idgrid[:, :], [[128, NT]], base=1, channel_multiplier=1),
                inc="GE")
            for tt in range(NT):
                P.op("gpsimd", lambda g, tt=tt: g.indirect_dma_start(
                    srcd.rearrange("a (g o) -> (a g) o", o=1),
                    IndirectOffsetOnAxis(dstcA[:, tt:tt + 1], 0),
                    idgrid[:, tt:tt + 1], None),
                    wait=[("HQ", hq_pf), ("GE", wio), ("VE", ve_dst)] if tt == 0
                    else [],
                    inc="AQ", amt=16)
            P.flush("phaseR")
        endR = P.snap()

        with ExitStack() as pdd:
            sbD = lambda n, sh, dt: pdd.enter_context(nc.sbuf_tensor(n, sh, dt))
            h1sb = sbD("h1sb", [128, NF * 128], BF16)
            gtm = sbD("gtm", [128, 2 * D], BF16)
            gfm = sbD("gfm", [128, 2 * D], BF16)
            ytile = sbD("ytile", [128, 2 * D], BF16)
            srcc = sbD("srcc", [128, 2], I32)
            gatec = sbD("gatec", [128, 2], F32)
            b1sb = sbD("b1sb", [128, NF * NE], F32)

            P.fence(endR, engines=P.ENGINES)
            lq_b1R = P.op("sync", lambda sy: sy.dma_start(b1sb[:, :], b1l[:, :]),
                          inc="LQ", amt=16)
            lq_dyn = max(lq_int, lq_b1R)
            ge_sc = P.op("gpsimd", lambda g: g.memset(srcc[:, :], 1 << 27))
            P.op("gpsimd", lambda g: g.memset(gtm[:, :], 0.0))
            P.op("gpsimd", lambda g: g.memset(gatec[:, :], 0.0))
            ge_sc = P.op("gpsimd", lambda g: g.memset(ytile[:, :], 0.0), inc="GE")
            P.fence({"GE": P.at("GE")}, engines=("sync", "tensor", "vector", "scalar"))
            P.flush("phaseDpre")
            endR2 = P.snap()


            def fn_sync(En):
                with ExitStack() as rc:
                    git = rc.enter_context(En.register("s_git"))
                    t1 = rc.enter_context(En.register("s_t1"))
                    pr_ = rc.enter_context(En.register("s_pr"))
                    lagP = rc.enter_context(En.register("s_lagP"))
                    nr = rc.enter_context(En.register("s_nr"))
                    En.reg_mov(git, 0)
                    En.reg_mov(lagP, 0)
                    for e in range(NE):
                        if e > 0:
                            En.wait_ge(P.h["EBPE"], e)
                            for ei in range(8):
                                En.dma_start(
                                    w1sb.rearrange("p (d f) -> p d f", f=DFF)[:, :, ei * 512:(ei + 1) * 512],
                                    w1bf[e].rearrange("(d pp) f -> pp d f", pp=128)[:, :, ei * 512:(ei + 1) * 512],
                                ).then_inc(P.h["WS"], 16)
                                En.dma_start(
                                    w2sb.rearrange("p (f m) -> p f m", m=D)[:, ei * 4:(ei + 1) * 4, :],
                                    w2bf[e].rearrange("(f pp) m -> pp f m", pp=128)[:, ei * 4:(ei + 1) * 4, :],
                                ).then_inc(P.h["WS"], 16)
                        En.wait_ge(P.h["LQ"], lq_dyn)
                        En.reg_load(nr, introw[0:1, 2 * e:2 * e + 1])
                        with En.Fori(0, nr) as it:
                            En.reg_div(t1, git, 2)
                            En.reg_add(t1, t1, 1)
                            En.reg_mul(t1, t1, 16)
                            En.wait_ge(P.h["DPE"], lagP)
                            En.reg_mod(pr_, git, 2)
                            with En.If_eq(pr_, 0):
                                En.wait_ge(P.h["GG0"], t1)
                                for dblk in range(DT):
                                    En.dma_start_transpose(
                                        gfm[:, dblk * 128:(dblk + 1) * 128],
                                        gtm[:, dblk * 128:(dblk + 1) * 128],
                                    ).then_inc(P.h["TQ"], 16)
                            with En.Else():
                                En.wait_ge(P.h["GG1"], t1)
                                for dblk in range(DT):
                                    En.dma_start_transpose(
                                        gfm[:, D + dblk * 128:D + (dblk + 1) * 128],
                                        gtm[:, D + dblk * 128:D + (dblk + 1) * 128],
                                    ).then_inc(P.h["TQ"], 16)
                            En.reg_mul(lagP, git, 33)
                            En.reg_add(git, git, 1)
                        En.sem_inc(P.h["EBSY"], 1)

            def fn_gpsimd(En):
                with ExitStack() as rc:
                    git = rc.enter_context(En.register("g_git"))
                    t1 = rc.enter_context(En.register("g_t1"))
                    pr_ = rc.enter_context(En.register("g_pr"))
                    nr = rc.enter_context(En.register("g_nr"))
                    offc = rc.enter_context(En.register("g_off"))
                    bc_r = rc.enter_context(En.register("g_bc"))
                    En.reg_mov(bc_r, S)
                    En.reg_mov(git, 0)
                    En.wait_ge(P.h["HQ"], endR["HQ"])
                    En.wait_ge(P.h["AQ"], endR["AQ"])
                    En.wait_ge(P.h["LQ"], lq_dyn)

                    def tilebody(par):
                        En.reg_add(t1, git, 1)
                        En.reg_div(t1, t1, 2)
                        En.reg_mul(t1, t1, 16)
                        En.wait_ge(P.h["GY%d" % par], t1)
                        En.dma_start(
                            srcc[:, par:par + 1],
                            bass.AP(srcd, offc, [[1, 128], [1, 1]]),
                        ).then_inc(P.h["GS"], 16)
                        En.reg_mul(t1, git, 16)
                        En.reg_add(t1, t1, 16)
                        En.wait_ge(P.h["GS"], t1)
                        En.reg_mul(t1, git, 128)
                        En.wait_ge(P.h["TQ"], t1)
                        En.indirect_dma_start(
                            gtm[:, par * D:(par + 1) * D], None,
                            h2tm[:, :],
                            IndirectOffsetOnAxis(srcc[:, par:par + 1], 0),
                            bounds_check=bc_r, oob_is_err=False,
                        ).then_inc(P.h["GG%d" % par], 16)
                        En.indirect_dma_start(
                            gatec[:, par:par + 1], None,
                            gateD.rearrange("a (g o) -> (a g) o", o=1),
                            IndirectOffsetOnAxis(srcc[:, par:par + 1], 0),
                            bounds_check=bc_r, oob_is_err=False,
                        ).then_inc(P.h["GC%d" % par], 16)
                        En.wait_ge(P.h["DVE"], git)
                        En.indirect_dma_start(
                            ytm[:, :],
                            IndirectOffsetOnAxis(srcc[:, 1 - par:2 - par], 0),
                            ytile[:, (1 - par) * D:(2 - par) * D], None,
                            bounds_check=bc_r, oob_is_err=False,
                        ).then_inc(P.h["GY%d" % (1 - par)], 16)

                    En.reg_load(nr, introw[0:1, 0:1])
                    for e in range(1, NE):
                        En.reg_load(t1, introw[0:1, 2 * e:2 * e + 1])
                        En.reg_add(nr, nr, t1)
                    En.reg_mov(offc, 0)
                    with En.Fori(0, nr) as it:
                        En.reg_mod(pr_, git, 2)
                        with En.If_eq(pr_, 0):
                            tilebody(0)
                        with En.Else():
                            tilebody(1)
                        En.reg_add(git, git, 1)
                        En.reg_add(offc, offc, 128)
                    En.sem_inc(P.h["EBGQ"], NE)
                    En.wait_ge(P.h["DVE"], git)
                    En.reg_mod(pr_, git, 2)
                    with En.If_eq(pr_, 0):
                        En.indirect_dma_start(
                            ytm[:, :], IndirectOffsetOnAxis(srcc[:, 1:2], 0),
                            ytile[:, D:2 * D], None,
                            bounds_check=bc_r, oob_is_err=False,
                        ).then_inc(P.h["GY1"], 16)
                    with En.Else():
                        En.indirect_dma_start(
                            ytm[:, :], IndirectOffsetOnAxis(srcc[:, 0:1], 0),
                            ytile[:, 0:D], None,
                            bounds_check=bc_r, oob_is_err=False,
                        ).then_inc(P.h["GY0"], 16)
                    En.reg_add(offc, git, 1)
                    En.reg_div(offc, offc, 2)
                    En.reg_mul(t1, offc, 16)
                    En.wait_ge(P.h["GY0"], t1)
                    En.reg_add(offc, offc, 1)
                    En.reg_sub(offc, offc, pr_)
                    En.reg_mul(t1, offc, 16)
                    En.wait_ge(P.h["GY1"], t1)
                    En.sem_inc(P.h["SDONE"], 1)

            def fn_pe(En):
                with ExitStack() as rc:
                    git = rc.enter_context(En.register("p_git"))
                    t1 = rc.enter_context(En.register("p_t1"))
                    t2 = rc.enter_context(En.register("p_t2"))
                    pr_ = rc.enter_context(En.register("p_pr"))
                    nr = rc.enter_context(En.register("p_nr"))
                    En.reg_mov(git, 0)
                    En.wait_ge(P.h["LQ"], lq_dyn)

                    def h1mm(f, par):
                        if f % 4 == 0:
                            En.wait_ge(P.h["WS"], 256 * e + 32 * (f // 4) + 16)
                        if f >= 2:
                            En.reg_add(t1, t2, f - 1)
                            En.wait_ge(P.h["DSE"], t1)
                        for d in range(DT):
                            mm = En.matmul(
                                pw0[:, (f % 2) * 512:(f % 2) * 512 + 128],
                                w1sb[:, d * DFF + f * 128:d * DFF + (f + 1) * 128],
                                gfm[:, par * D + d * 128:par * D + (d + 1) * 128],
                                start=(d == 0), stop=(d == DT - 1),
                            )
                            if d == DT - 1:
                                mm.then_inc(P.h["DPE"], 1)

                    def yemm(f):
                        if f % 4 == 0:
                            En.wait_ge(P.h["WS"], 256 * e + 32 * (f // 4) + 32)
                        En.reg_add(t1, t2, f + 1)
                        En.wait_ge(P.h["DSE"], t1)
                        for h in range(2):
                            mm = En.matmul(
                                (pb4, pb5)[h][:, 0:CH],
                                h1sb[:, f * 128:(f + 1) * 128],
                                w2sb[:, f * D + h * 512:f * D + (h + 1) * 512],
                                start=(f == 0), stop=(f == NF - 1),
                            )
                            if f == NF - 1 and h == 1:
                                mm.then_inc(P.h["DPE"], 1)

                    def tilecompute(par):
                        h1mm(0, par)
                        for f in range(1, NF):
                            h1mm(f, par)
                            yemm(f - 1)
                        yemm(NF - 1)

                    for e in range(NE):
                        En.reg_load(nr, introw[0:1, 2 * e:2 * e + 1])
                        with En.Fori(0, nr) as it:
                            En.reg_mul(t1, git, 128)
                            En.reg_add(t1, t1, 128)
                            En.wait_ge(P.h["TQ"], t1)
                            En.wait_ge(P.h["DVE"], git)
                            En.reg_mul(t2, git, 32)
                            En.reg_mod(pr_, git, 2)
                            with En.If_eq(pr_, 0):
                                tilecompute(0)
                            with En.Else():
                                tilecompute(1)
                            En.reg_add(git, git, 1)
                        En.sem_inc(P.h["EBPE"], 1)

            def fn_scalar(En):
                with ExitStack() as rc:
                    git = rc.enter_context(En.register("c_git"))
                    t1 = rc.enter_context(En.register("c_t1"))
                    t2 = rc.enter_context(En.register("c_t2"))
                    nr = rc.enter_context(En.register("c_nr"))
                    En.reg_mov(git, 0)
                    En.wait_ge(P.h["LQ"], lq_dyn)
                    for e in range(NE):
                        En.reg_load(nr, introw[0:1, 2 * e:2 * e + 1])
                        with En.Fori(0, nr) as it:
                            En.reg_mul(t2, git, 33)
                            for f in range(NF):
                                En.reg_add(t1, t2, f + 1)
                                En.wait_ge(P.h["DPE"], t1)
                                En.activation(
                                    h1sb[:, f * 128:(f + 1) * 128],
                                    pw0[:, (f % 2) * 512:(f % 2) * 512 + 128],
                                    AF.Relu,
                                    bias=b1sb[:, f * NE + e:f * NE + e + 1],
                                ).then_inc(P.h["DSE"], 1)
                            En.reg_add(git, git, 1)
                        En.sem_inc(P.h["EBSE"], 1)

            def fn_vector(En):
                with ExitStack() as rc:
                    git = rc.enter_context(En.register("v_git"))
                    t1 = rc.enter_context(En.register("v_t1"))
                    pr_ = rc.enter_context(En.register("v_pr"))
                    lagS = rc.enter_context(En.register("v_lagS"))
                    nr = rc.enter_context(En.register("v_nr"))
                    En.reg_mov(git, 0)
                    En.reg_mov(lagS, 0)
                    En.wait_ge(P.h["LQ"], lq_dyn)

                    def evac(par):
                        for h in range(2):
                            op = En.tensor_scalar(
                                ytile[:, par * D + h * 512:par * D + (h + 1) * 512],
                                (pb4, pb5)[h][:, 0:CH],
                                gatec[:, par:par + 1], None, ALU.mult,
                            )
                            if h == 1:
                                op.then_inc(P.h["DVE"], 1)

                    for e in range(NE):
                        En.reg_load(nr, introw[0:1, 2 * e:2 * e + 1])
                        with En.Fori(0, nr) as it:
                            En.reg_mul(t1, git, 33)
                            En.reg_add(t1, t1, 33)
                            En.wait_ge(P.h["DPE"], t1)
                            En.reg_div(t1, git, 2)
                            En.reg_add(t1, t1, 1)
                            En.reg_mul(t1, t1, 16)
                            En.reg_add(lagS, git, 1)
                            En.reg_div(lagS, lagS, 2)
                            En.reg_mul(lagS, lagS, 16)
                            En.reg_mod(pr_, git, 2)
                            with En.If_eq(pr_, 0):
                                En.wait_ge(P.h["GC0"], t1)
                                En.wait_ge(P.h["GY0"], lagS)
                                evac(0)
                            with En.Else():
                                En.wait_ge(P.h["GC1"], t1)
                                En.wait_ge(P.h["GY1"], lagS)
                                evac(1)
                            En.reg_add(git, git, 1)
                        En.sem_inc(P.h["EBVE"], 1)

            with nc.Block("phaseDdyn") as dblk:
                dblk.gpsimd(fn_gpsimd)
                dblk.sync(fn_sync)
                dblk.tensor(fn_pe)
                dblk.scalar(fn_scalar)
                dblk.vector(fn_vector)
            P.cnt["EBPE"] += NE
            P.cnt["EBSY"] += NE
            P.cnt["EBGQ"] += NE
            P.cnt["EBSE"] += NE
            P.cnt["EBVE"] += NE
            P.cnt["SDONE"] += 1
            P.cnt["WS"] += 256 * (NE - 1)

            if debug_x1:
                dbg_ye = nc.dram_tensor("dbg_ye", [128, 2 * CH], F32,
                                        kind="ExternalOutput")
                dbg_yt = nc.dram_tensor("dbg_yt", [128, 2 * D], BF16,
                                        kind="ExternalOutput")
                dbg_gfm = nc.dram_tensor("dbg_gfm", [128, 2 * D], BF16,
                                         kind="ExternalOutput")
                dbg_h1 = nc.dram_tensor("dbg_h1", [128, NF * 128], BF16,
                                        kind="ExternalOutput")
                dbg_w1 = nc.dram_tensor("dbg_w1", [128, DT * DFF], BF16,
                                        kind="ExternalOutput")
                dbg_w2 = nc.dram_tensor("dbg_w2", [128, NF * D], BF16,
                                        kind="ExternalOutput")
                yecp = pdd.enter_context(nc.sbuf_tensor("yecp", [128, 2 * CH], F32))
                P.waitonly("vector", [("SDONE", 1)])
                P.waitonly("sync", [("SDONE", 1)])
                ve_ye = P.op("vector", lambda v: v.tensor_copy(
                    yecp[:, 0:CH], pb4[:, 0:CH]), inc="VE")
                ve_ye = P.op("vector", lambda v: v.tensor_copy(
                    yecp[:, CH:2 * CH], pb5[:, 0:CH]),
                    wait=[("VE", ve_ye)], inc="VE")
                P.op("sync", lambda sy: sy.dma_start(dbg_ye[:, :], yecp[:, :]),
                     wait=[("VE", ve_ye)], inc="HQ", amt=16)
                P.op("sync", lambda sy: sy.dma_start(dbg_yt[:, :], ytile[:, :]),
                     inc="HQ", amt=16)
                P.op("sync", lambda sy: sy.dma_start(dbg_gfm[:, :], gfm[:, :]),
                     inc="HQ", amt=16)
                P.op("sync", lambda sy: sy.dma_start(dbg_h1[:, :], h1sb[:, :]),
                     inc="HQ", amt=16)
                P.op("sync", lambda sy: sy.dma_start(dbg_w1[:, :], w1sb[:, :]),
                     inc="HQ", amt=16)
                P.op("sync", lambda sy: sy.dma_start(dbg_w2[:, :], w2sb[:, :]),
                     inc="HQ", amt=16)
                P.flush("dbgdump")

            with ExitStack() as pe_:
                sbE = lambda n, sh, dt: pe_.enter_context(nc.sbuf_tensor(n, sh, dt))
                yfm = sbE("yfmE", [128, 2 * 4096], BF16)
                P.waitonly("sync", [("SDONE", 1)])
                P.waitonly("gpsimd", [("SDONE", 1)])
                aq_e = {0: 0, 1: 0}
                tq_e = {0: 0, 1: 0}
                NG = S // 512
                for g in range(NG):
                    half = g % 2
                    for dblk2 in range(DT):
                        tq_e[half] = P.op("sync", lambda sy, g=g, half=half, dblk2=dblk2: sy.dma_start_transpose(
                            yfm[:, half * 4096 + dblk2 * 512: half * 4096 + (dblk2 + 1) * 512],
                            ytm[1 + 512 * g:1 + 512 * (g + 1), dblk2 * 128:(dblk2 + 1) * 128]),
                            wait=(([("AQ%d" % half, aq_e[half])] if dblk2 == 0 else []) +
                                  ([("HQ", P.at("HQ"))] if g < 2 and dblk2 == 0
                                   else [])),
                            inc="TQ%d" % half, amt=16)
                    aq_e[half] = P.op("gpsimd", lambda gp, g=g, half=half: gp.dma_start(
                        r3(outT)[:, :, 512 * g:512 * (g + 1)],
                        yfm[:, half * 4096:(half + 1) * 4096].rearrange("p (d c) -> p d c", c=512),
                        accum_op=ALU.add),
                        wait=[("TQ%d" % half, tq_e[half]), ("OQ", oq_base)],
                        inc="AQ%d" % half, amt=16)
                P.waitonly("sync", [("OQ", P.at("OQ")), ("AQ0", P.at("AQ0")),
                                    ("AQ1", P.at("AQ1")), ("AQ", P.at("AQ")),
                                    ("LQ", P.at("LQ")), ("HQ", P.at("HQ"))])
                P.flush("phaseE")

    return nc


def _consts(S):
    t = np.arange(S, dtype=np.float32)
    i2 = np.arange(0, DH, 2, dtype=np.float32)
    scale = (1.0 / (10000.0 ** (i2 / DH) + np.float32(np.finfo(np.float32).eps))).astype(np.float32)
    ang = (t[:, None] * scale[None, :]).astype(np.float32)
    pos = np.stack([np.sin(ang), np.cos(ang)], axis=-1).reshape(S, DH).astype(np.float32)
    pos2 = np.concatenate([np.ascontiguousarray(pos.T)] * 2, axis=0)
    kk = np.arange(128)[:, None]
    qq = np.arange(CH)[None, :]
    masks = np.concatenate(
        [np.where(128 * r + kk <= qq, 0.0, NEG).astype(np.float32) for r in range(4)],
        axis=1)
    ident = np.eye(128, dtype=np.float32)
    return np.ascontiguousarray(pos2), np.ascontiguousarray(masks), ident


def kernel(L0_attn_x, L0_attn_Wq, L0_attn_Wk, L0_attn_Wv, L0_attn_Wo,
           L0_ffn_Wr, L0_ffn_br, L0_ffn_W1, L0_ffn_b1, L0_ffn_W2, L0_ffn_b2):
    x = np.asarray(L0_attn_x, dtype=np.float32)
    B, S, _ = x.shape
    pos2, masks, ident = _consts(S)
    b1 = np.asarray(L0_ffn_b1, dtype=np.float32)
    b1l = np.ascontiguousarray(
        b1.reshape(NE, NF, 128).transpose(2, 1, 0).reshape(128, NF * NE))
    shared = {
        "wq": np.ascontiguousarray(L0_attn_Wq, dtype=np.float32),
        "wk": np.ascontiguousarray(L0_attn_Wk, dtype=np.float32),
        "wv": np.ascontiguousarray(L0_attn_Wv, dtype=np.float32),
        "wo": np.ascontiguousarray(L0_attn_Wo, dtype=np.float32),
        "wr": np.ascontiguousarray(L0_ffn_Wr, dtype=np.float32),
        "brr": np.ascontiguousarray(
            np.asarray(L0_ffn_br, dtype=np.float32).reshape(1, NE)),
        "w1": np.ascontiguousarray(L0_ffn_W1, dtype=np.float32),
        "b1l": b1l,
        "w2": np.ascontiguousarray(L0_ffn_W2, dtype=np.float32),
        "b2": np.ascontiguousarray(L0_ffn_b2, dtype=np.float32),
        "pos2": pos2, "masksd": masks, "identd": ident,
        "ltri": np.ascontiguousarray(np.triu(np.ones((8, 8), dtype=np.float32), 1)),
    }
    nc = build(S)
    in_maps = []
    for i in range(B):
        m = dict(shared)
        m["xT"] = np.ascontiguousarray(x[i].T)
        in_maps.append(m)
    res = run_bass_kernel_spmd(nc, in_maps, core_ids=list(range(8)))
    out = np.stack([np.ascontiguousarray(np.asarray(r["outT"]).T)
                    for r in res.results], axis=0)
    return out.astype(np.float32)
